# version 04
# baseline (speedup 1.0000x reference)
import math
import numpy as np
import ml_dtypes
import concourse.bass as bass
import concourse.mybir as mybir
from concourse.bass_utils import run_bass_kernel_spmd

F32 = mybir.dt.float32
BF16 = mybir.dt.bfloat16
FP16 = mybir.dt.float16
I32 = mybir.dt.int32
AF = mybir.ActivationFunctionType
ALU = mybir.AluOpType
AX = mybir.AxisListType

B, T, D = 16, 2048, 1024
NCORES = 8
SEQ_PER_CORE = B // NCORES
NT = T // 128
NC = D // 128


COMPUTE_ENGS = ("te", "ve", "se", "ge", "sq")
NDQ = 16
NGQ = 8
DQ_DOMS = tuple(f"dq{i}" for i in range(NDQ))
GQ_DOMS = tuple(f"gq{i}" for i in range(NGQ))
ALL_DOMS = COMPUTE_ENGS + DQ_DOMS + GQ_DOMS


class KB:

    def __init__(self, nc):
        self.nc = nc
        self.tasks = {e: [] for e in COMPUTE_ENGS}
        self.cnt = {d: 0 for d in ALL_DOMS}
        self.seen = {e: {d: 0 for d in ALL_DOMS} for e in COMPUTE_ENGS}
        self.clock = {}
        self.mem = {}
        self.sems = {}
        self.nwaits = 0
        self._seen_insts = {}

    def _region(self, key):
        r = self.mem.get(key)
        if r is None:
            r = {"w": None, "r": {}}
            self.mem[key] = r
        return r

    def _need(self, eng, dom, tick):
        return self.seen[eng].get(dom, 0) < tick

    def _absorb(self, eng, dom, tick):
        s = self.seen[eng]
        if s.get(dom, 0) < tick:
            s[dom] = tick
        c = self.clock.get((dom, tick))
        if c:
            for d2, t2 in c.items():
                if s.get(d2, 0) < t2:
                    s[d2] = t2

    def task(self, eng, fn, reads=(), writes=(), tag=None):
        deps = {}
        for key in reads:
            r = self._region(key)
            if r["w"] is not None:
                d, t = r["w"]
                if deps.get(d, 0) < t:
                    deps[d] = t
        for key in writes:
            r = self._region(key)
            if r["w"] is not None:
                d, t = r["w"]
                if deps.get(d, 0) < t:
                    deps[d] = t
            for d, t in r["r"].items():
                if deps.get(d, 0) < t:
                    deps[d] = t
        waits = []
        for d, t in sorted(deps.items()):
            if self._need(eng, d, t):
                waits.append((d, t))
                self._absorb(eng, d, t)
        self.nwaits += len(waits)
        self.tasks[eng].append((waits, fn, None, tag))
        self.cnt[eng] += 1
        tick = self.cnt[eng]
        self.clock[(eng, tick)] = dict(self.seen[eng], **{eng: tick})
        for key in reads:
            self._region(key)["r"][eng] = tick
        for key in writes:
            r = self._region(key)
            r["w"] = (eng, tick)
            r["r"] = {}
        return (eng, tick)

    def dma_task(self, eng, fn, reads=(), writes=(), tag=None):
        doms = DQ_DOMS if eng == "sq" else GQ_DOMS
        idx_attr = "_rr_dq" if eng == "sq" else "_rr_gq"
        rr = getattr(self, idx_attr, 0)
        setattr(self, idx_attr, rr + 1)
        dom = doms[rr % len(doms)]
        deps = {}
        prev = self.cnt[dom]
        if prev > 0:
            deps[dom] = prev
        for key in reads:
            r = self._region(key)
            if r["w"] is not None:
                d, t = r["w"]
                if deps.get(d, 0) < t:
                    deps[d] = t
        for key in writes:
            r = self._region(key)
            if r["w"] is not None:
                d, t = r["w"]
                if deps.get(d, 0) < t:
                    deps[d] = t
            for d, t in r["r"].items():
                if deps.get(d, 0) < t:
                    deps[d] = t
        waits = []
        for d, t in sorted(deps.items()):
            if self._need(eng, d, t):
                waits.append((d, t))
                self._absorb(eng, d, t)
        self.nwaits += len(waits)
        self.tasks[eng].append((waits, fn, dom, tag))
        n_dma = getattr(fn, "n_dma", 1)
        self.cnt[dom] += 16 * n_dma
        tick = self.cnt[dom]
        self.clock[(dom, tick)] = dict(self.seen[eng], **{dom: tick})
        for key in reads:
            self._region(key)["r"][dom] = tick
        for key in writes:
            r = self._region(key)
            r["w"] = (dom, tick)
            r["r"] = {}
        return (dom, tick)

    def fence(self, engs=("te", "ve", "se", "ge")):
        snap = {p: self.cnt[p] for p in engs}
        know = {p: dict(self.seen[p], **{p: self.cnt[p]}) for p in engs}
        for e in engs:
            waits = [(p, snap[p]) for p in engs
                     if p != e and snap[p] > 0 and self.seen[e].get(p, 0) < snap[p]]
            if waits:
                self.tasks[e].append((waits, ("fence",), None, "fence"))
            s = self.seen[e]
            for p in engs:
                if p == e:
                    continue
                for d2, t2 in know[p].items():
                    if d2 != e and s.get(d2, 0) < t2:
                        s[d2] = t2

    def emit(self):
        nc = self.nc
        sems = {}
        import contextlib
        self._es = contextlib.ExitStack()
        for d in ALL_DOMS:
            sems[d] = self._es.enter_context(nc.semaphore(f"tick_{d}"))
        self.sems = sems
        block = self._es.enter_context(nc.Block())
        kb = self

        fence_sem = self._es.enter_context(nc.semaphore("fence_sem"))

        def run_stream(en, eng_name):
            for waits, fn, dma_dom, tag in kb.tasks[eng_name]:
                if isinstance(fn, tuple) and fn[0] == "fence":
                    for d, t in waits:
                        en.wait_ge(sems[d], t)
                    continue
                for d, t in waits:
                    en.wait_ge(sems[d], t)
                ret = fn(en)
                if isinstance(ret, list):
                    assert dma_dom is not None, f"list-return on non-dma task {tag}"
                    exp = getattr(fn, "n_dma", 1)
                    assert len(ret) == exp, f"task {tag}: n_dma={exp} but emitted {len(ret)}"
                    for inst in ret:
                        inst.then_inc(sems[dma_dom], 16)
                elif ret is not None:
                    ret.then_inc(sems[eng_name], 1)
                else:
                    raise RuntimeError(f"task on {eng_name} returned None (tag={tag})")

        block.tensor(lambda en: run_stream(en, "te"))
        block.vector(lambda en: run_stream(en, "ve"))
        block.scalar(lambda en: run_stream(en, "se"))
        block.gpsimd(lambda en: run_stream(en, "ge"))

        def sync_body(en):
            run_stream(en, "sq")
            for d in ALL_DOMS:
                if d != "sq" and self.cnt[d] > 0:
                    en.wait_ge(sems[d], self.cnt[d])
        block.sync(sync_body)
        self._es.close()



def _t(a):
    return np.ascontiguousarray(np.asarray(a).T)


def _col128(a):
    a = np.asarray(a).reshape(-1)
    return np.ascontiguousarray(a.reshape(-1, 128).T)


CONST_SCALARS = [1.0 / 1024.0, 1e-5, 1.0 / 2048.0, 1e-6, 1e-12, 2.0, 0.5,
                 -1.0, 1.0, 1.0 / 32.0, 0.125, 12582912.0, -12582912.0,
                 0.001, 0.1, 1.0 / 16.0, 6.0, 1.0 / 6.0]


def host_prep(inputs):
    ins = {k: np.asarray(v) for k, v in inputs.items()}
    sh = {}
    sh["c_scal"] = np.broadcast_to(
        np.asarray(CONST_SCALARS, np.float32)[None, :], (128, len(CONST_SCALARS))).copy()
    sh["c_ident_bf"] = np.eye(128, dtype=ml_dtypes.bfloat16)
    sh["c_ident_f32"] = np.eye(8, dtype=np.float32)
    sh["c_ident_f32"] = np.eye(8, dtype=np.float32)
    sh["c_ones_bf"] = np.ones((1, 512), dtype=ml_dtypes.bfloat16)
    sh["c_ones_col"] = np.ones((128, 1), dtype=ml_dtypes.bfloat16)
    sh["c_ones_f32"] = np.ones((1, 128), dtype=np.float32)
    am = np.full((128, 128), -1e30, dtype=np.float32)
    sh["c_allmask"] = am
    sh["c_zero8"] = np.zeros((128, 8), dtype=np.float32)
    ind = np.zeros((16, 16 * 128), dtype=ml_dtypes.bfloat16)
    for n in range(16):
        ind[n, n * 128:(n + 1) * 128] = 1.0
    sh["c_ind16"] = ind
    hh2 = np.zeros((128, 2), dtype=ml_dtypes.bfloat16)
    hh2[0:64, 0] = 1.0
    hh2[64:128, 1] = 1.0
    sh["c_hh2"] = hh2
    m = np.zeros((128, 128), dtype=np.float32)
    m[np.tril_indices(128, -1)] = -1e30
    sh["c_diagmask"] = m
    dh = 64
    inv = (10000.0 ** (-(np.arange(0, dh, 2, dtype=np.float64) / dh))).astype(np.float64)
    pos = np.arange(T, dtype=np.float64)
    ang = pos[None, :] * inv[:, None]
    cos = np.cos(ang).astype(np.float32)
    sin = np.sin(ang).astype(np.float32)
    sh["c_rope_cos"] = np.ascontiguousarray(np.concatenate([cos] * 4, 0)).astype(ml_dtypes.bfloat16)
    sh["c_rope_sin"] = np.ascontiguousarray(np.concatenate([sin] * 4, 0)).astype(ml_dtypes.bfloat16)
    R = np.zeros((128, 128), dtype=np.float32)
    for p in range(128):
        h64 = (p // 64) * 64
        k = p % 64
        if k < 32:
            R[p, h64 + k + 32] = -1.0
        else:
            R[p, h64 + k - 32] = 1.0
    sh["c_rot"] = np.ascontiguousarray(R.T).astype(ml_dtypes.bfloat16)
    for n in ("Wq", "Wk", "Wv", "Wo"):
        sh[f"L0_{n}"] = ins[f"L0_attn_{n}"]
    for n in ("bq", "bk", "bv", "bo"):
        sh[f"L0_{n}"] = ins[f"L0_attn_{n}"].reshape(1, -1)
        sh[f"L0_{n}_c"] = _col128(ins[f"L0_attn_{n}"])
    sh["L0_gwT"] = _t(ins["L0_ffn_gate_w"])
    sh["L0_beta"] = ins["L0_ffn_beta"].reshape(1, -1)
    sh["L0_gate_proj"] = ins["L0_ffn_gate_proj_w"]
    sh["L0_up_proj"] = ins["L0_ffn_up_proj_w"]
    sh["L0_down_proj"] = ins["L0_ffn_down_proj_w"]
    sh["L0_sh_gate"] = ins["L0_ffn_shared_gate_w"][0]
    sh["L0_sh_up"] = ins["L0_ffn_shared_up_w"][0]
    sh["L0_sh_down"] = ins["L0_ffn_shared_down_w"][0]
    sh["L1_wqkv"] = ins["L1_attn_w_qkv"]
    bqkv = ins["L1_attn_b_qkv"].reshape(-1)
    sh["L1_bq_c"] = _col128(bqkv[0:1024])
    sh["L1_bk_c"] = _col128(bqkv[1024:2048])
    sh["L1_bv"] = bqkv[2048:3072].reshape(1, -1)
    sh["L1_wproj"] = ins["L1_attn_w_proj"]
    sh["L1_bproj"] = ins["L1_attn_b_proj"].reshape(1, -1)
    sh["L1_WrT"] = _t(ins["L1_ffn_W_r"])
    sh["L1_WkT"] = _t(ins["L1_ffn_W_k"])
    sh["L1_WvT"] = _t(ins["L1_ffn_W_v"])
    sh["L1_mur"] = _col128(ins["L1_ffn_mu_r"])
    sh["L1_muk"] = _col128(ins["L1_ffn_mu_k"])
    sh["L2_normw"] = _col128(ins["L2_attn_norm_w"])
    wq = ins["L2_attn_w_qkv"].copy()
    perm = np.concatenate([np.arange(0, 64, 2), np.arange(1, 64, 2)])
    idx = np.arange(3 * D)
    for h in range(16):
        for qk in range(2):
            base = qk * D + h * 64
            idx[base:base + 64] = base + perm
    sh["L2_wqkv"] = np.ascontiguousarray(wq[:, idx])
    pmk = ins["L2_attn_pm_k"]
    pmk = pmk[:, :, perm]
    pmkT = np.zeros((128, 8 * 16), dtype=np.float32)
    for g in range(8):
        for hh in range(2):
            pmkT[hh * 64:(hh + 1) * 64, g * 16:(g + 1) * 16] = pmk[2 * g + hh].T
    sh["L2_pmkT"] = pmkT
    pmv = ins["L2_attn_pm_v"]
    sh["L2_pmv"] = np.ascontiguousarray(
        pmv.transpose(1, 0, 2).reshape(16, 16 * 64))
    sh["L2_wout"] = ins["L2_attn_w_out"]
    sh["L2_bout"] = ins["L2_attn_b_out"].reshape(1, -1)
    sh["L2_wiT"] = _t(ins["L2_ffn_wi_w"])
    sh["L2_woT"] = _t(ins["L2_ffn_wo_w"])
    sh["L2_AsT"] = np.ascontiguousarray(ins["L2_ffn_lora_As"].transpose(0, 2, 1))
    sh["L2_BsT"] = np.ascontiguousarray(ins["L2_ffn_lora_Bs"].transpose(0, 2, 1))
    sh["L2_topk_i"] = ins["L2_ffn_top_k_indices"].reshape(1, 2).astype(np.int32)
    sh["L2_topk_v"] = ins["L2_ffn_top_k_values"].reshape(1, 2)
    sh["L3_inpT"] = _t(ins["L3_attn_in_proj_w"])
    sh["L3_convw"] = np.ascontiguousarray(
        ins["L3_attn_conv_w"].reshape(2048, 4).reshape(16, 128, 4).transpose(1, 0, 2).reshape(128, 64))
    sh["L3_convb"] = _col128(ins["L3_attn_conv_b"])
    sh["L3_xprojT"] = _t(ins["L3_attn_x_proj_w"])
    sh["L3_dtw"] = _col128(ins["L3_attn_dt_proj_w"].reshape(-1))
    sh["L3_dtb"] = _col128(ins["L3_attn_dt_proj_b"])
    la = ins["L3_attn_log_A"]
    sh["L3_logA"] = np.ascontiguousarray(
        la.reshape(16, 128, 16).transpose(1, 0, 2).reshape(128, 256))
    sh["L3_D"] = _col128(ins["L3_attn_D_param"])
    sh["L3_normw"] = _col128(ins["L3_attn_norm_w"])
    sh["L3_normb"] = _col128(ins["L3_attn_norm_b"])
    sh["L3_outpT"] = _t(ins["L3_attn_out_proj_w"])
    sh["L3_wg_nat"] = ins["L3_ffn_w_gate"]
    sh["L3_wu_nat"] = ins["L3_ffn_w_up"]
    sh["L3_wd_nat"] = ins["L3_ffn_w_down"]
    sh = {k: np.ascontiguousarray(v) for k, v in sh.items()}

    x = ins["L0_attn_y_all"]
    maps = []
    for c in range(NCORES):
        m = dict(sh)
        m["x_in"] = np.ascontiguousarray(x[2 * c: 2 * c + 2])
        maps.append(m)
    return maps


DT_MAP = {np.dtype(np.float32): F32, np.dtype(ml_dtypes.bfloat16): BF16,
          np.dtype(np.int32): I32}



class Build:
    def __init__(self, nc, host_map, debug_tensors=()):
        self.nc = nc
        self.kb = KB(nc)
        self.es = None
        self.dram = {}
        for name, arr in host_map.items():
            self.dram[name] = nc.dram_tensor(
                name, list(arr.shape), DT_MAP[arr.dtype], kind="ExternalInput").ap()
        self.out = nc.dram_tensor("out", [SEQ_PER_CORE, T, D], F32,
                                  kind="ExternalOutput").ap()
        self.wgq_d = nc.dram_tensor("wgq_d", [1024, 2816], BF16, kind="Internal").ap()
        self.wuq_d = nc.dram_tensor("wuq_d", [1024, 2816], BF16, kind="Internal").ap()
        self.wdq_d = nc.dram_tensor("wdq_d", [2816, 1024], BF16, kind="Internal").ap()
        self.debug = {}
        for (name, shape, dt) in debug_tensors:
            self.debug[name] = nc.dram_tensor(name, list(shape), dt,
                                              kind="ExternalOutput").ap()
        self.sb = {}
        self.psum = {}
        self._uid = 0

    def uid(self):
        self._uid += 1
        return self._uid

    def alloc_persist(self, es):
        nc = self.nc
        sb = self.sb
        sb["x"] = es.enter_context(nc.sbuf_tensor("x", [128, NT * D], F32))
        sb["br"] = es.enter_context(nc.sbuf_tensor("br", [128, NT * D], BF16))
        sb["xnfm"] = es.enter_context(nc.sbuf_tensor("xnfm", [128, NC * T], BF16))
        sb["ident_bf"] = es.enter_context(nc.sbuf_tensor("ident_bf", [128, 128], BF16))
        sb["ones_bf"] = es.enter_context(nc.sbuf_tensor("ones_bf", [1, 512], BF16))
        sb["ones_col"] = es.enter_context(nc.sbuf_tensor("ones_col", [128, 1], BF16))
        sb["ones_f32"] = es.enter_context(nc.sbuf_tensor("ones_f32", [1, 128], F32))
        sb["ident_f32"] = es.enter_context(nc.sbuf_tensor("ident_f32", [8, 8], F32))
        sb["zero8"] = es.enter_context(nc.sbuf_tensor("zero8", [128, 8], F32))
        sb["hh2"] = es.enter_context(nc.sbuf_tensor("hh2", [128, 2], BF16))
        sb["diagmask"] = es.enter_context(nc.sbuf_tensor("diagmask", [128, 128], F32))
        sb["allmask"] = es.enter_context(nc.sbuf_tensor("allmask", [128, 128], F32))
        sb["stats"] = es.enter_context(nc.sbuf_tensor("stats", [128, 64], F32))
        sb["c_scal"] = es.enter_context(
            nc.sbuf_tensor("sb_c_scal", [128, len(CONST_SCALARS)], F32))
        sb["lnscr"] = es.enter_context(nc.sbuf_tensor("lnscr", [128, 1024], BF16))
        sb["lntile"] = es.enter_context(nc.sbuf_tensor("lntile", [128, 2 * D], BF16))
        sb["rows"] = es.enter_context(nc.sbuf_tensor("rows", [1, 524], F32))
        for i in range(6):
            self.psum[f"pb{i}"] = es.enter_context(
                nc.psum_tensor(f"pb{i}", [128, 512], F32))
        for i in range(2):
            self.psum[f"pt{i}"] = es.enter_context(
                nc.psum_tensor(f"pt{i}", [128, 1024], BF16))

    def cs(self, value):
        i = CONST_SCALARS.index(value)
        return self.sb["c_scal"][:, i:i + 1]

    def load_consts(self):
        kb = self.kb
        sb = self.sb
        keys = []
        for name, key in (("c_ident_bf", "ident_bf"), ("c_ones_col", "ones_col"),
                          ("c_ones_f32", "ones_f32"), ("c_ident_f32", "ident_f32"),
                          ("c_zero8", "zero8"), ("c_hh2", "hh2"),
                          ("c_ones_bf", "ones_bf"), ("c_diagmask", "diagmask"),
                          ("c_allmask", "allmask"), ("c_scal", "c_scal")):
            src = self.dram[name]
            dst = sb[key]
            kb.dma_task("sq", lambda en, s=src, d=dst: [en.dma_start(out=d[:], in_=s)],
                        writes=(key,), tag=f"load_{key}")
            keys.append(key)
        for e in COMPUTE_ENGS:
            if e != "sq":
                kb.task(e, lambda en: en.nop(nofuse=True), reads=tuple(keys), tag="c_rdy")

    def load_x(self, seq):
        kb = self.kb
        x = self.sb["x"]
        xin = self.dram["x_in"]
        for i in range(4):
            def fn(en, i=i, seq=seq):
                src = xin[seq].rearrange("(t p) d -> p t d", p=128)[:, 4 * i:4 * (i + 1), :]
                return [en.dma_start(out=x[:, i * 4 * D:(i + 1) * 4 * D], in_=src)]
            kb.dma_task("sq", fn, writes=(f"x{i}",), tag=f"load_x{i}")

    def store_out(self, seq):
        kb = self.kb
        x = self.sb["x"]
        for i in range(4):
            def fn(en, i=i, seq=seq):
                dst = self.out[seq].rearrange("(t p) d -> p t d", p=128)[:, 4 * i:4 * (i + 1), :]
                return [en.dma_start(out=dst, in_=x[:, i * 4 * D:(i + 1) * 4 * D])]
            kb.dma_task("sq", fn,
                        reads=tuple(f"x{j}" for j in (i,)), tag=f"store_x{i}")

    def xkey(self, t):
        return f"x{t // 4}"

    def emit_ln(self, tag, sink=None, post_stats=None):
        kb = self.kb
        x = self.sb["x"]
        stats = self.sb["stats"]
        scr = self.sb["lnscr"]
        if sink is None:
            sink = self.fm_sink()
        for t in range(NT):
            xt = x[:, t * D:(t + 1) * D]
            xk = self.xkey(t)
            def f1(en, xt=xt, t=t):
                return en.reduce_sum(stats[:, t:t + 1], xt, axis=AX.X)
            kb.task("ve", f1, reads=(xk,), writes=(f"stS{t}",), tag=f"{tag}_sum{t}")
        for t in range(NT):
            xt = x[:, t * D:(t + 1) * D]
            xk = self.xkey(t)
            def f2(en, t=t):
                return en.tensor_scalar(out=stats[:, 48 + t:49 + t], in0=stats[:, t:t + 1],
                                        scalar1=-1.0 / D, scalar2=None, op0=ALU.mult)
            kb.task("ve", f2, reads=(f"stS{t}",), writes=(f"stM{t}",), tag=f"{tag}_nm{t}")
            def f3(en, xt=xt, t=t):
                return en.activation(out=scr[:], in_=xt, func=AF.Square,
                                     bias=stats[:, 48 + t:49 + t], scale=1.0,
                                     accum_out=stats[:, 16 + t:17 + t])
            kb.task("se", f3, reads=(xk, f"stM{t}"), writes=("lnscr", f"stQ{t}"),
                    tag=f"{tag}_sq{t}")
        allQ = tuple(f"stQ{t}" for t in range(NT))
        allM = tuple(f"stM{t}" for t in range(NT))
        def g1(en):
            return en.tensor_scalar(out=stats[:, 16:32], in0=stats[:, 16:32],
                                    scalar1=1.0 / D, scalar2=1e-5,
                                    op0=ALU.mult, op1=ALU.add)
        kb.task("ve", g1, reads=allQ, writes=("stV",) + allQ, tag=f"{tag}_veps")
        def g2(en):
            return en.activation(out=stats[:, 32:48], in_=stats[:, 16:32], func=AF.Sqrt)
        kb.task("se", g2, reads=("stV",), writes=("stD",), tag=f"{tag}_std")
        def g3(en):
            return en.reciprocal(stats[:, 32:48], stats[:, 32:48])
        kb.task("ve", g3, reads=("stD",), writes=("stR0",), tag=f"{tag}_rs0")
        def g4(en):
            return en.activation(out=stats[:, 0:16], in_=stats[:, 32:48], func=AF.Square)
        kb.task("se", g4, reads=("stR0",),
                writes=("stT1",) + tuple(f"stS{t}" for t in range(NT)),
                tag=f"{tag}_nsq")
        def g5(en):
            return en.tensor_tensor(out=stats[:, 0:16], in0=stats[:, 0:16],
                                    in1=stats[:, 16:32], op=ALU.mult)
        kb.task("ve", g5, reads=("stT1", "stV"), writes=("stT1",), tag=f"{tag}_nmul")
        def g6(en):
            return en.tensor_scalar(out=stats[:, 0:16], in0=stats[:, 0:16],
                                    scalar1=-0.5, scalar2=1.5, op0=ALU.mult, op1=ALU.add)
        kb.task("ve", g6, reads=("stT1",), writes=("stT1",), tag=f"{tag}_naff")
        def g7(en):
            return en.tensor_tensor(out=stats[:, 32:48], in0=stats[:, 32:48],
                                    in1=stats[:, 0:16], op=ALU.mult)
        kb.task("ve", g7, reads=("stT1", "stR0"), writes=("stB",), tag=f"{tag}_nfin")
        if post_stats is not None:
            post_stats(kb, stats)
        def g8(en):
            return en.tensor_tensor(out=stats[:, 48:64], in0=stats[:, 48:64],
                                    in1=stats[:, 32:48], op=ALU.mult)
        kb.task("ve", g8, reads=("stB",) + allM, writes=("stB2",) + allM,
                tag=f"{tag}_b2")
        buf = self.sb["lntile"]
        for t in range(NT):
            xt = x[:, t * D:(t + 1) * D]
            xk = self.xkey(t)
            half = t % 2
            dst = buf[:, half * D:(half + 1) * D]
            dkey = f"lntile{half}"
            def f7(en, xt=xt, t=t, dst=dst):
                return en.activation(out=dst, in_=xt, func=AF.Identity,
                                     bias=stats[:, 48 + t:49 + t],
                                     scale=stats[:, 32 + t:33 + t])
            kb.task("se", f7, reads=(xk, "stB2", "stB"), writes=(dkey,),
                    tag=f"{tag}_tm{t}")
            sink(t, dst, dkey)

    def fm_sink(self, dst_name="xnfm", keyf=None):
        kb = self.kb
        ident = self.sb["ident_bf"]
        dst = self.sb[dst_name]
        if keyf is None:
            keyf = lambda c, t: f"{dst_name}{c}b{t // 2}"
        def sink(t, src_ap, src_key):
            pt = self.psum[f"pt{t % 2}"]
            ptk = f"pt{t % 2}"
            def ftr(en, t=t, src_ap=src_ap):
                last = None
                for c in range(NC):
                    last = en.transpose(pt[:, c * 128:(c + 1) * 128],
                                        src_ap[:, c * 128:(c + 1) * 128],
                                        ident[:])
                return last
            kb.task("te", ftr, reads=(src_key,), writes=(ptk,),
                    tag=f"tr{t}")
            def fev(en, t=t):
                full = dst[:]
                out = bass.AP(full.tensor, t * 128,
                              [full.ap[0], [T, NC], [1, 128]])
                return en.tensor_copy(out=out, in_=pt[:, 0:NC * 128])
            kb.task("ve", fev, reads=(ptk,),
                    writes=tuple(keyf(c, t) for c in range(NC)),
                    tag=f"trev{t}")
        return sink

    def emit_tm_to_fm(self, tag, src_tm, src_key, dst_fm, dst_key, nt, nct, rowlen):
        kb = self.kb
        ident = self.sb["ident_bf"]
        ncols = nt * 128
        for c in range(nct):
            for half in range((nt + 7) // 8):
                tlo = half * 8
                thi = min(nt, tlo + 8)
                pt = self.psum[f"pt{half % 2}"]
                ptk = f"pt{half % 2}"
                def ftr(en, c=c, tlo=tlo, thi=thi, pt=pt):
                    last = None
                    for j in range(thi - tlo):
                        t = tlo + j
                        last = en.transpose(
                            pt[:, j * 128:(j + 1) * 128],
                            src_tm[:, t * rowlen + c * 128: t * rowlen + c * 128 + 128],
                            ident[:])
                    return last
                kb.task("te", ftr,
                        reads=tuple(f"{src_key}{t}" for t in range(tlo, thi)) + ("ident_bf",),
                        writes=(ptk,), tag=f"{tag}_tr{c}_{half}")
                def fev(en, c=c, tlo=tlo, thi=thi, pt=pt):
                    n = (thi - tlo) * 128
                    return en.tensor_copy(
                        out=dst_fm[:, c * ncols + tlo * 128: c * ncols + tlo * 128 + n],
                        in_=pt[:, 0:n])
                kb.task("ve", fev, reads=(ptk,), writes=(f"{dst_key}{c}",),
                        tag=f"{tag}_trev{c}_{half}")



    def xnk(self, c, lo, n):
        return tuple(f"xnfm{c}b{i}" for i in range(lo // 256, (lo + n + 255) // 256))

    def emit_l0_attn(self, seq):
        import contextlib
        nc, kb = self.nc, self.kb
        xnfm = self.sb["xnfm"]
        rows = self.sb["rows"]
        br = self.sb["br"]
        onesr = self.sb["ones_bf"]
        onesc = self.sb["ones_col"]
        pb = [self.psum[f"pb{i}"] for i in range(6)]

        self.kb.fence()
        with contextlib.ExitStack() as es:
            K = es.enter_context(nc.sbuf_tensor(f"l0K_{seq}", [128, NC * T], BF16))
            expT = es.enter_context(nc.sbuf_tensor(f"l0e_{seq}", [128, 16 * 512], BF16))
            scr = es.enter_context(nc.sbuf_tensor(f"l0s_{seq}", [128, 512], BF16))
            biasc = es.enter_context(nc.sbuf_tensor(f"l0b_{seq}", [128, 32], F32))
            negb = es.enter_context(nc.sbuf_tensor(f"l0n_{seq}", [128, 1], F32))
            zb = es.enter_context(nc.sbuf_tensor(f"l0zb_{seq}", [128, 512], F32))

            self.emit_ln(f"l0ln_{seq}")

            def fb(en):
                return [en.dma_start(out=biasc[:, 0:8], in_=self.dram["L0_bq_c"]),
                        en.dma_start(out=biasc[:, 8:16], in_=self.dram["L0_bk_c"])]
            fb.n_dma = 2
            kb.dma_task("ge", fb, writes=("l0biasc",), tag="l0_biasload")
            def fbs(en):
                return en.tensor_scalar(out=biasc[:, 16:24], in0=biasc[:, 0:8],
                                        scalar1=1.0 / 32.0, scalar2=None, op0=ALU.mult)
            kb.task("ve", fbs, reads=("l0biasc",), writes=("l0biasq",), tag="l0_bscale")

            with contextlib.ExitStack() as esA:
                kwbuf = esA.enter_context(nc.sbuf_tensor(f"l0kw_{seq}", [128, 2048], BF16))
                vwbuf = esA.enter_context(nc.sbuf_tensor(f"l0vw_{seq}", [128, 4096], BF16))
                bvbo = esA.enter_context(nc.sbuf_tensor(f"l0bv_{seq}", [1, 1024], BF16))
                def fbv(en):
                    return [en.dma_start(out=bvbo[0:1, 0:1024], in_=self.dram["L0_bv"])]
                kb.dma_task("ge", fbv, writes=("l0bvbo",), tag="l0_bvload")

                wk = self.dram["L0_Wk"].rearrange("(ci p) o -> p ci o", p=128)
                for co in range(8):
                    sl = co % 2
                    wap = kwbuf[:, sl * 1024: sl * 1024 + 1024]
                    wkey = f"l0kw{sl}"
                    def fw(en, co=co, wap=wap):
                        return [en.dma_start(out=wap.rearrange("p (ci o) -> p ci o", ci=8),
                                             in_=wk[:, :, co * 128:(co + 1) * 128])]
                    kb.dma_task("ge", fw, writes=(wkey,), tag=f"l0_wk{co}")
                    for tc in range(4):
                        ps, psk = pb[tc % 2], f"pb{tc % 2}"
                        def fm(en, co=co, tc=tc, ps=ps, wap=wap):
                            last = None
                            for ci in range(8):
                                last = en.matmul(ps[:, 0:512],
                                                 wap[:, ci * 128:(ci + 1) * 128],
                                                 xnfm[:, ci * T + tc * 512: ci * T + tc * 512 + 512],
                                                 start=(ci == 0), stop=(ci == 7))
                            return last
                        rd = (wkey,) + tuple(k for ci in range(8) for k in self.xnk(ci, tc * 512, 512))
                        kb.task("te", fm, reads=rd, writes=(psk,), tag=f"l0_kmm{co}_{tc}")
                        def fe(en, co=co, tc=tc, ps=ps):
                            return en.activation(out=K[:, co * T + tc * 512: co * T + tc * 512 + 512],
                                                 in_=ps[:, 0:512], func=AF.Identity,
                                                 bias=biasc[:, 8 + co:9 + co], scale=1.0)
                        kb.task("se", fe, reads=(psk, "l0biasc"), writes=(f"K{co}",),
                                tag=f"l0_kev{co}_{tc}")

                wv = self.dram["L0_Wv"].rearrange("(ci p) o -> p ci o", p=128)
                for dblk in range(2):
                    def fw(en, dblk=dblk):
                        return [en.dma_start(out=vwbuf[:].rearrange("p (ci o) -> p ci o", ci=8),
                                             in_=wv[:, :, dblk * 512:(dblk + 1) * 512])]
                    kb.dma_task("ge", fw, writes=("l0vw",), tag=f"l0_wv{dblk}")
                    for t in range(NT):
                        ps, psk = pb[2 + t % 2], f"pb{2 + t % 2}"
                        def fm(en, dblk=dblk, t=t, ps=ps):
                            en.matmul(ps[:, 0:512], onesr[0:1, 0:128],
                                      bvbo[0:1, dblk * 512:(dblk + 1) * 512],
                                      start=True, stop=False)
                            last = None
                            for ci in range(8):
                                last = en.matmul(ps[:, 0:512],
                                                 xnfm[:, ci * T + t * 128: ci * T + t * 128 + 128],
                                                 vwbuf[:, ci * 512:(ci + 1) * 512],
                                                 start=False, stop=(ci == 7))
                            return last
                        rd = ("l0vw", "l0bvbo") + tuple(k for ci in range(8) for k in self.xnk(ci, t * 128, 128))
                        kb.task("te", fm, reads=rd, writes=(psk,), tag=f"l0_vmm{dblk}_{t}")
                        def fe(en, dblk=dblk, t=t, ps=ps):
                            return en.tensor_copy(
                                out=br[:, t * D + dblk * 512: t * D + dblk * 512 + 512],
                                in_=ps[:, 0:512])
                        kb.task("ve", fe, reads=(psk,), writes=(f"br{t}",),
                                tag=f"l0_vev{dblk}_{t}")

                for tc in range(4):
                    ps, psk = pb[4 + tc % 2], f"pb{4 + tc % 2}"
                    for c in range(8):
                        def fsq(en, c=c, tc=tc):
                            return en.activation(out=scr[:, 0:512],
                                                 in_=K[:, c * T + tc * 512: c * T + tc * 512 + 512],
                                                 func=AF.Square)
                        kb.task("se", fsq, reads=(f"K{c}",), writes=("l0scr",), tag=f"l0_ksq{tc}_{c}")
                        def fm(en, c=c, ps=ps):
                            return en.matmul(ps[0:1, 0:512], onesc[:, 0:1],
                                             scr[:, 0:512], start=(c == 0), stop=(c == 7))
                        kb.task("te", fm, reads=("l0scr",), writes=(psk,),
                                tag=f"l0_knmm{tc}_{c}")
                    def fe(en, tc=tc, ps=ps):
                        return en.reduce_max(rows[0:1, 512 + tc:513 + tc],
                                             ps[0:1, 0:512], axis=AX.X)
                    kb.task("ve", fe, reads=(psk,), writes=("rows_kn",), tag=f"l0_knev{tc}")
                def fmx(en):
                    return en.reduce_max(rows[0:1, 516:517], rows[0:1, 512:516], axis=AX.X)
                kb.task("ve", fmx, reads=("rows_kn",), writes=("rows_mk",), tag="l0_knmax")

            kb.fence()
            with contextlib.ExitStack() as esB:
                wqcol = esB.enter_context(nc.sbuf_tensor(f"l0qw_{seq}", [128, 2048], BF16))
                qstg = esB.enter_context(nc.sbuf_tensor(f"l0qs_{seq}", [128, 8 * 512], BF16))
                wq = self.dram["L0_Wq"].rearrange("(ci p) o -> p ci o", p=128)
                for tc in range(4):
                    for co in range(8):
                        sl = co % 2
                        wap = wqcol[:, sl * 1024: sl * 1024 + 1024]
                        wkey = f"l0qw{sl}"
                        def fw(en, co=co, wap=wap):
                            return [en.dma_start(out=wap.rearrange("p (ci o) -> p ci o", ci=8),
                                                 in_=wq[:, :, co * 128:(co + 1) * 128])]
                        kb.dma_task("ge", fw, writes=(wkey,), tag=f"l0_wq{tc}_{co}")
                        ps, psk = pb[co % 2], f"pb{co % 2}"
                        def fm(en, tc=tc, co=co, ps=ps, wap=wap):
                            last = None
                            for ci in range(8):
                                last = en.matmul(
                                    ps[:, 0:512],
                                    wap[:, ci * 128:(ci + 1) * 128],
                                    xnfm[:, ci * T + tc * 512: ci * T + tc * 512 + 512],
                                    start=(ci == 0), stop=(ci == 7))
                            return last
                        rd = (wkey,) + tuple(k for ci in range(8) for k in self.xnk(ci, tc * 512, 512))
                        kb.task("te", fm, reads=rd, writes=(psk,), tag=f"l0_qmm{tc}_{co}")
                        def fe(en, co=co, ps=ps):
                            return en.activation(
                                out=qstg[:, co * 512: co * 512 + 512],
                                in_=ps[:, 0:512], func=AF.Identity,
                                bias=biasc[:, 16 + co:17 + co], scale=self.cs(1.0 / 32.0))
                        kb.task("se", fe, reads=(psk, "l0biasq"), writes=(f"qstg{co}",),
                                tag=f"l0_qev{tc}_{co}")
                    def fcp(en, tc=tc):
                        full = xnfm[:]
                        out = bass.AP(full.tensor, tc * 512, [full.ap[0], [T, NC], [1, 512]])
                        return en.tensor_copy(out=out, in_=qstg[:, 0:8 * 512])
                    kb.task("ve", fcp,
                            reads=tuple(f"qstg{co}" for co in range(8)),
                            writes=tuple(f"xnfm{c}b{2 * tc + i}" for c in range(NC) for i in range(2)),
                            tag=f"l0_qcp{tc}")

            for b in range(4):
                ps5 = pb[5]
                for c in range(8):
                    def fsq(en, c=c, b=b):
                        return en.activation(out=scr[:, 0:512],
                                             in_=xnfm[:, c * T + b * 512: c * T + b * 512 + 512],
                                             func=AF.Square)
                    kb.task("se", fsq, reads=(f"xnfm{c}b{2*b}", f"xnfm{c}b{2*b+1}"),
                            writes=("l0scr",), tag=f"l0_qsq{b}_{c}")
                    def fm(en, c=c):
                        return en.matmul(ps5[0:1, 0:512], onesc[:, 0:1],
                                         scr[:, 0:512], start=(c == 0), stop=(c == 7))
                    kb.task("te", fm, reads=("l0scr",), writes=("pb5",),
                            tag=f"l0_qnmm{b}_{c}")
                def fq2(en):
                    return en.reduce_max(rows[0:1, 517:518], ps5[0:1, 0:512], axis=AX.X)
                kb.task("ve", fq2, reads=("pb5",), writes=("rows_mq",), tag=f"l0_qnmax{b}")
                def fq3(en):
                    return en.tensor_tensor(out=rows[0:1, 518:519], in0=rows[0:1, 517:518],
                                            in1=rows[0:1, 516:517], op=ALU.mult)
                kb.task("ve", fq3, reads=("rows_mq", "rows_mk"), writes=("rows_b2",),
                        tag=f"l0_bnd1{b}")
                def fq4(en):
                    return en.activation(out=rows[0:1, 519:520], in_=rows[0:1, 518:519],
                                         func=AF.Sqrt)
                kb.task("se", fq4, reads=("rows_b2",), writes=("rows_b3",), tag=f"l0_bnd2{b}")
                def fq5(en):
                    return en.tensor_scalar(out=rows[0:1, 520:521], in0=rows[0:1, 519:520],
                                            scalar1=-1.0, scalar2=None, op0=ALU.mult)
                kb.task("ve", fq5, reads=("rows_b3",), writes=("rows_b4",), tag=f"l0_bnd3{b}")
                def fq6(en):
                    return en.matmul(pb[4][:, 0:1], self.sb["ones_f32"][0:1, 0:128],
                                     rows[0:1, 520:521], start=True, stop=True)
                kb.task("te", fq6, reads=("rows_b4",), writes=("pb4",), tag=f"l0_bnd4{b}")
                def fq7(en):
                    return en.tensor_copy(out=negb[:, 0:1], in_=pb[4][:, 0:1])
                kb.task("ve", fq7, reads=("pb4",), writes=("l0negb",), tag=f"l0_bnd5{b}")

                nj = 4 * b + 4
                for j in range(nj):
                    ps, psk = pb[j % 2], f"pb{j % 2}"
                    def fm(en, b=b, j=j, ps=ps):
                        last = None
                        for ci in range(8):
                            last = en.matmul(
                                ps[:, 0:512],
                                K[:, ci * T + j * 128: ci * T + j * 128 + 128],
                                xnfm[:, ci * T + b * 512: ci * T + b * 512 + 512],
                                start=(ci == 0), stop=(ci == 7))
                        return last
                    rd = tuple(f"K{ci}" for ci in range(8)) + tuple(
                        f"xnfm{ci}b{q}" for ci in range(8) for q in (2 * b, 2 * b + 1))
                    kb.task("te", fm, reads=rd, writes=(psk,), tag=f"l0_smm{b}_{j}")
                    for c2 in range(4):
                        off = b * 512 + c2 * 128 - j * 128
                        if off > 0:
                            continue
                        mask = self.sb["diagmask"] if off == 0 else self.sb["allmask"]
                        def fmask(en, c2=c2, ps=ps, mask=mask):
                            return en.tensor_tensor(out=ps[:, c2 * 128:(c2 + 1) * 128],
                                                    in0=ps[:, c2 * 128:(c2 + 1) * 128],
                                                    in1=mask[:], op=ALU.add)
                        kb.task("ve", fmask, reads=(psk,), writes=(psk,),
                                tag=f"l0_msk{b}_{j}_{c2}")
                    def fexp(en, j=j, ps=ps):
                        return en.activation(out=expT[:, j * 512:(j + 1) * 512],
                                             in_=ps[:, 0:512], func=AF.Exp,
                                             bias=negb[:, 0:1], scale=1.0)
                    kb.task("se", fexp, reads=(psk, "l0negb"), writes=(f"expT{j}",),
                            tag=f"l0_exp{b}_{j}")
                def fz(en, nj=nj):
                    last = None
                    for j in range(nj):
                        last = en.matmul(pb[4][0:1, 0:512], onesc[:, 0:1],
                                         expT[:, j * 512:(j + 1) * 512],
                                         start=(j == 0), stop=(j == nj - 1))
                    return last
                kb.task("te", fz, reads=tuple(f"expT{j}" for j in range(nj)),
                        writes=("pb4",), tag=f"l0_zmm{b}")
                def fz1(en):
                    return en.tensor_copy(out=rows[0:1, 0:512], in_=pb[4][0:1, 0:512])
                kb.task("ve", fz1, reads=("pb4",), writes=("rows_z",), tag=f"l0_zev{b}")
                def fz2(en):
                    return en.reciprocal(rows[0:1, 0:512], rows[0:1, 0:512])
                kb.task("ve", fz2, reads=("rows_z",), writes=("rows_z",), tag=f"l0_zrec{b}")
                def fz3(en):
                    return en.matmul(pb[4][:, 0:512], self.sb["ones_f32"][0:1, 0:128],
                                     rows[0:1, 0:512], start=True, stop=True)
                kb.task("te", fz3, reads=("rows_z",), writes=("pb4",), tag=f"l0_zbc{b}")
                def fz4(en):
                    return en.tensor_copy(out=zb[:, 0:512], in_=pb[4][:, 0:512])
                kb.task("ve", fz4, reads=("pb4",), writes=("l0zb",), tag=f"l0_zbe{b}")
                for mc in range(8):
                    ps, psk = pb[2 + mc % 2], f"pb{2 + mc % 2}"
                    def fm(en, mc=mc, nj=nj, ps=ps):
                        last = None
                        for j in range(nj):
                            last = en.matmul(ps[:, 0:512],
                                             br[:, j * D + mc * 128: j * D + mc * 128 + 128],
                                             expT[:, j * 512:(j + 1) * 512],
                                             start=(j == 0), stop=(j == nj - 1))
                        return last
                    rd = tuple(f"expT{j}" for j in range(nj)) + tuple(f"br{j}" for j in range(nj))
                    kb.task("te", fm, reads=rd, writes=(psk,), tag=f"l0_cmm{b}_{mc}")
                    def fe(en, b=b, mc=mc, ps=ps):
                        return en.tensor_tensor(
                            out=xnfm[:, mc * T + b * 512: mc * T + b * 512 + 512],
                            in0=ps[:, 0:512], in1=zb[:, 0:512], op=ALU.mult)
                    kb.task("ve", fe, reads=(psk, "l0zb"),
                            writes=(f"xnfm{mc}b{2*b}", f"xnfm{mc}b{2*b+1}"),
                            tag=f"l0_cev{b}_{mc}")

        self.emit_outproj(seq, "L0_Wo", "L0_bo", "l0o")

    def emit_outproj(self, seq, wname, bname, tag):
        import contextlib
        nc, kb = self.nc, self.kb
        xnfm = self.sb["xnfm"]
        br = self.sb["br"]
        onesr = self.sb["ones_bf"]
        pb = [self.psum[f"pb{i}"] for i in range(6)]
        self.kb.fence()
        with contextlib.ExitStack() as es:
            wo_sb = es.enter_context(nc.sbuf_tensor(f"{tag}w_{seq}", [128, 8 * 1024], BF16))
            bo_bf = es.enter_context(nc.sbuf_tensor(f"{tag}b_{seq}", [1, 1024], BF16))
            wo = self.dram[wname].rearrange("(ci p) o -> p ci o", p=128)
            def fwo(en):
                return [en.dma_start(out=wo_sb[:].rearrange("p (ci o) -> p ci o", ci=8), in_=wo)]
            kb.dma_task("ge", fwo, writes=(f"{tag}wo",), tag=f"{tag}_wload")
            if bname is not None:
                def fbo(en):
                    return [en.dma_start(out=bo_bf[:], in_=self.dram[bname])]
                kb.dma_task("ge", fbo, writes=(f"{tag}bo",), tag=f"{tag}_bload")
            for t in range(NT):
                for h in range(2):
                    ps, psk = pb[(2 * t + h) % 4], f"pb{(2 * t + h) % 4}"
                    def fm(en, t=t, h=h, ps=ps):
                        if bname is not None:
                            en.matmul(ps[:, 0:512], onesr[0:1, 0:128],
                                      bo_bf[0:1, h * 512: h * 512 + 512],
                                      start=True, stop=False)
                        last = None
                        for mc in range(8):
                            last = en.matmul(
                                ps[:, 0:512],
                                xnfm[:, mc * T + t * 128: mc * T + t * 128 + 128],
                                wo_sb[:, mc * 1024 + h * 512: mc * 1024 + h * 512 + 512],
                                start=(bname is None and mc == 0), stop=(mc == 7))
                        return last
                    rd = (f"{tag}wo", f"{tag}bo") + tuple(f"xnfm{mc}b{t // 2}" for mc in range(8))
                    kb.task("te", fm, reads=rd, writes=(psk,), tag=f"{tag}_omm{t}_{h}")
                    def fe(en, t=t, h=h, ps=ps):
                        return en.tensor_copy(
                            out=br[:, t * D + h * 512: t * D + h * 512 + 512],
                            in_=ps[:, 0:512])
                    kb.task("ve", fe, reads=(psk,), writes=(f"br{t}",),
                            tag=f"{tag}_oev{t}_{h}")

    def emit_l0_ffn(self, seq):
        import contextlib
        nc, kb = self.nc, self.kb
        xnfm = self.sb["xnfm"]
        x = self.sb["x"]
        br = self.sb["br"]
        stats = self.sb["stats"]
        pb = [self.psum[f"pb{i}"] for i in range(6)]
        identf = self.sb["ident_f32"]

        self.kb.fence()
        with contextlib.ExitStack() as es:
            gw = es.enter_context(nc.sbuf_tensor(f"f0gw_{seq}", [128, 192], F32))
            gwh = es.enter_context(nc.sbuf_tensor(f"f0gh_{seq}", [128, 128], BF16))
            lgtm = es.enter_context(nc.sbuf_tensor(f"f0lt_{seq}", [128, 128], F32))
            cmat = es.enter_context(nc.sbuf_tensor(f"f0cm_{seq}", [128, 128], F32))
            rscr = es.enter_context(nc.sbuf_tensor(f"f0rs_{seq}", [128, 32], F32))
            esR = contextlib.ExitStack()
            logt = esR.enter_context(nc.sbuf_tensor(f"f0lg_{seq}", [8, 2048], F32))
            lofm = esR.enter_context(nc.sbuf_tensor(f"f0lo_{seq}", [128, 8 * 512], BF16))
            ltile = self.sb["lntile"]

            self.emit_ln(f"f0ln_{seq}")

            gwr = self.dram["L0_gwT"].rearrange("(ci p) e -> p ci e", p=128)
            def fgw(en):
                return [en.dma_start(out=gw[:, 0:64].rearrange("p (ci e) -> p ci e", ci=8), in_=gwr)]
            kb.dma_task("ge", fgw, writes=("f0gw",), tag="f0_gw")
            def fgh(en):
                return en.tensor_copy(out=gwh[:, 0:64], in_=gw[:, 0:64])
            kb.task("ve", fgh, reads=("f0gw",), writes=("f0gwh",), tag="f0_gwh")
            def fgl(en):
                return en.tensor_tensor(out=gw[:, 64:128], in0=gw[:, 0:64],
                                        in1=gwh[:, 0:64], op=ALU.subtract)
            kb.task("ve", fgl, reads=("f0gw", "f0gwh"), writes=("f0gwl0",), tag="f0_gwl0")
            def fgl2(en):
                return en.tensor_copy(out=gwh[:, 64:128], in_=gw[:, 64:128])
            kb.task("ve", fgl2, reads=("f0gwl0",), writes=("f0gwl",), tag="f0_gwl")
            def fbe(en):
                return [en.dma_start(out=gw[0:1, 128:136], in_=self.dram["L0_beta"])]
            kb.dma_task("ge", fbe, writes=("f0бе",), tag="f0_beta")
            def fbb(en):
                return en.matmul(pb[5][:, 0:8], self.sb["ones_f32"][0:1, 0:128],
                                 gw[0:1, 128:136], start=True, stop=True)
            kb.task("te", fbb, reads=("f0бе",), writes=("pb5",), tag="f0_betab")
            def fbc(en):
                return en.tensor_copy(out=gw[:, 136:144], in_=pb[5][:, 0:8])
            kb.task("ve", fbc, reads=("pb5",), writes=("f0betab",), tag="f0_betac")

            for tq in range(4):
                for i in range(4):
                    t = tq * 4 + i
                    half = t % 2
                    lt = ltile[:, half * D:(half + 1) * D]
                    ltk = f"lntile{half}"
                    def fhi(en, t=t, lt=lt):
                        return en.activation(out=lt, in_=x[:, t * D:(t + 1) * D],
                                             func=AF.Identity,
                                             bias=stats[:, 48 + t:49 + t],
                                             scale=stats[:, 32 + t:33 + t])
                    kb.task("se", fhi, reads=(self.xkey(t), "stB2", "stB"), writes=(ltk,),
                            tag=f"f0_hi{t}")
                    def flo1(en, t=t, lt=lt):
                        return en.scalar_tensor_tensor(
                            out=self.sb["lnscr"][:, 0:1024], in0=x[:, t * D:(t + 1) * D],
                            scalar=stats[:, 32 + t:33 + t], in1=lt,
                            op0=ALU.mult, op1=ALU.subtract)
                    kb.task("ve", flo1, reads=(self.xkey(t), ltk, "stB"),
                            writes=("lnscr",), tag=f"f0_lo1{t}")
                    def flo2(en, t=t, lt=lt):
                        return en.tensor_scalar(out=lt, in0=self.sb["lnscr"][:, 0:1024],
                                                scalar1=stats[:, 48 + t:49 + t], scalar2=None,
                                                op0=ALU.add)
                    kb.task("ve", flo2, reads=("lnscr", "stB2"), writes=(ltk,),
                            tag=f"f0_lo2{t}")
                    pt = self.psum[f"pt{t % 2}"]
                    ptk = f"pt{t % 2}"
                    def ftr(en, t=t, lt=lt, pt=pt):
                        last = None
                        for c in range(NC):
                            last = en.transpose(pt[:, c * 128:(c + 1) * 128],
                                                lt[:, c * 128:(c + 1) * 128],
                                                self.sb["ident_bf"][:])
                        return last
                    kb.task("te", ftr, reads=(ltk,), writes=(ptk,), tag=f"f0_lotr{t}")
                    def fev(en, i=i, pt=pt):
                        full = lofm[:]
                        out = bass.AP(full.tensor, i * 128, [full.ap[0], [512, NC], [1, 128]])
                        return en.tensor_copy(out=out, in_=pt[:, 0:NC * 128])
                    kb.task("ve", fev, reads=(ptk,), writes=(f"f0lofm{i}",),
                            tag=f"f0_loev{t}")
                def flg(en, tq=tq):
                    last = None
                    for ci in range(8):
                        en.matmul(pb[4][0:8, 0:512], gwh[:, ci * 8:(ci + 1) * 8],
                                  xnfm[:, ci * T + tq * 512: ci * T + tq * 512 + 512],
                                  start=(ci == 0), stop=False)
                    for ci in range(8):
                        en.matmul(pb[4][0:8, 0:512], gwh[:, 64 + ci * 8: 64 + ci * 8 + 8],
                                  xnfm[:, ci * T + tq * 512: ci * T + tq * 512 + 512],
                                  start=False, stop=False)
                    for ci in range(8):
                        last = en.matmul(pb[4][0:8, 0:512], gwh[:, ci * 8:(ci + 1) * 8],
                                         lofm[:, ci * 512:(ci + 1) * 512],
                                         start=False, stop=(ci == 7))
                    return last
                rd = ("f0gwh", "f0gwl") + tuple(f"f0lofm{i}" for i in range(4)) + tuple(
                    k for ci in range(8) for k in self.xnk(ci, tq * 512, 512))
                kb.task("te", flg, reads=rd, writes=("pb4",), tag=f"f0_lg{tq}")
                def fle(en, tq=tq):
                    return en.tensor_copy(out=logt[0:8, tq * 512:(tq + 1) * 512],
                                          in_=pb[4][0:8, 0:512])
                kb.task("ve", fle, reads=("pb4",), writes=(f"f0logt{tq}",), tag=f"f0_lge{tq}")
            for half in range(2):
                pt = self.psum[f"pt{half}"]
                ptk = f"pt{half}"
                ps = pb[half]
                psk = f"pb{half}"
                def ftr(en, half=half, ps=ps):
                    last = None
                    for i in range(8):
                        t = half * 8 + i
                        last = en.transpose(ps[:, i * 8:(i + 1) * 8],
                                            logt[0:8, t * 128:(t + 1) * 128],
                                            identf[:])
                    return last
                kb.task("te", ftr, reads=tuple(f"f0logt{tq}" for tq in range(4)),
                        writes=(psk,), tag=f"f0_lgtr{half}")
                def fev(en, half=half, ps=ps):
                    return en.tensor_copy(out=lgtm[:, half * 64:(half + 1) * 64],
                                          in_=ps[:, 0:64])
                kb.task("ve", fev, reads=(psk,), writes=("f0lgtm",), tag=f"f0_lgte{half}")

            zero8 = self.sb["zero8"]
            for t in range(NT):
                lg = lgtm[:, t * 8:(t + 1) * 8]
                cm = cmat[:, t * 8:(t + 1) * 8]
                r0, r1, r2, r3 = rscr[:, 0:8], rscr[:, 8:16], rscr[:, 16:24], rscr[:, 24:32]
                m1 = rscr[:, 16:17]
                def q1(en, lg=lg):
                    return en.tensor_tensor(out=r0, in0=lg, in1=gw[:, 136:144], op=ALU.add)
                kb.task("ve", q1, reads=("f0lgtm", "f0betab"), writes=("f0r0",), tag=f"f0rt{t}a")
                for k in range(2):
                    def q2(en):
                        return en.reduce_max(rscr[:, 16:17], r0, axis=AX.X)
                    kb.task("ve", q2, reads=("f0r0",), writes=("f0m1",), tag=f"f0rt{t}b{k}")
                    def q3(en):
                        return en.tensor_scalar(out=r1, in0=r0, scalar1=rscr[:, 16:17],
                                                scalar2=None, op0=ALU.is_ge)
                    kb.task("ve", q3, reads=("f0r0", "f0m1"), writes=("f0r1",), tag=f"f0rt{t}c{k}")
                    def q4(en):
                        return en.tensor_tensor_scan(out=r2, data0=r1, data1=zero8[:, 0:8],
                                                     initial=0.0, op0=ALU.add, op1=ALU.add)
                    kb.task("ve", q4, reads=("f0r1",), writes=("f0r2",), tag=f"f0rt{t}d{k}")
                    def q5(en):
                        return en.tensor_scalar(out=r2, in0=r2, scalar1=1.0, scalar2=None,
                                                op0=ALU.is_equal)
                    kb.task("ve", q5, reads=("f0r2",), writes=("f0r2",), tag=f"f0rt{t}e{k}")
                    def q6(en):
                        return en.tensor_tensor(out=r1, in0=r1, in1=r2, op=ALU.mult)
                    kb.task("ve", q6, reads=("f0r1", "f0r2"), writes=("f0r1",), tag=f"f0rt{t}f{k}")
                    def q7(en, lg=lg):
                        return en.tensor_tensor(out=r2, in0=r1, in1=lg, op=ALU.mult)
                    kb.task("ve", q7, reads=("f0r1", "f0lgtm"), writes=("f0r2",), tag=f"f0rt{t}g{k}")
                    def q8(en):
                        return en.reduce_sum(rscr[:, 24:25], r2, axis=AX.X)
                    kb.task("ve", q8, reads=("f0r2",), writes=("f0r3",), tag=f"f0rt{t}h{k}")
                    def q9(en):
                        return en.activation(out=rscr[:, 25:26], in_=rscr[:, 24:25],
                                             func=AF.Sigmoid)
                    kb.task("se", q9, reads=("f0r3",), writes=("f0p",), tag=f"f0rt{t}i{k}")
                    if k == 0:
                        def q10(en, cm=cm):
                            return en.tensor_scalar(out=cm, in0=r1, scalar1=rscr[:, 25:26],
                                                    scalar2=None, op0=ALU.mult)
                        kb.task("ve", q10, reads=("f0r1", "f0p"), writes=(f"f0cm{t}",), tag=f"f0rt{t}j{k}")
                        def q11(en):
                            return en.scalar_tensor_tensor(out=r0, in0=r1, scalar=-1e30,
                                                           in1=r0, op0=ALU.mult, op1=ALU.add)
                        kb.task("ve", q11, reads=("f0r1", "f0r0"), writes=("f0r0",), tag=f"f0rt{t}k{k}")
                    else:
                        def q12(en, cm=cm):
                            return en.scalar_tensor_tensor(out=cm, in0=r1, scalar=rscr[:, 25:26],
                                                           in1=cm, op0=ALU.mult, op1=ALU.add)
                        kb.task("ve", q12, reads=("f0r1", "f0p", f"f0cm{t}"),
                                writes=(f"f0cm{t}",), tag=f"f0rt{t}j{k}")

            esR.close()
            kb.fence()
            comb = es.enter_context(nc.sbuf_tensor(f"f0c_{seq}", [128, 8 * 1024], F32))
            hfm = es.enter_context(nc.sbuf_tensor(f"f0h_{seq}", [128, 16 * 512], BF16))
            wg = es.enter_context(nc.sbuf_tensor(f"f0wg_{seq}", [128, 4 * 1024], BF16))
            wu = es.enter_context(nc.sbuf_tensor(f"f0wu_{seq}", [128, 4 * 1024], BF16))
            wd = es.enter_context(nc.sbuf_tensor(f"f0wd_{seq}", [128, 512], BF16))
            sil = es.enter_context(nc.sbuf_tensor(f"f0si_{seq}", [128, 512], BF16))
            gp, up, dp = self.dram["L0_gate_proj"], self.dram["L0_up_proj"], self.dram["L0_down_proj"]
            sg, su, sd = self.dram["L0_sh_gate"], self.dram["L0_sh_up"], self.dram["L0_sh_down"]
            for th in range(2):
                for i in range(2):
                    def fz(en, i=i):
                        return en.memset(comb[:, i * 4096:(i + 1) * 4096], 0.0)
                    kb.task("ve", fz, writes=tuple(f"f0comb{i*4+j}" for j in range(4)),
                            tag=f"f0_cz{th}_{i}")
                for e in range(9):
                    for tq2 in range(2):
                        tqg = th * 2 + tq2
                        for mc in range(16):
                            sl = mc % 2
                            wgap = wg[:, sl * 2048: sl * 2048 + 1024]
                            wuap = wu[:, sl * 2048: sl * 2048 + 1024]
                            def fwl(en, e=e, mc=mc, wgap=wgap, wuap=wuap):
                                if e < 8:
                                    g_src = gp[e].rearrange("(ci p) m -> p ci m", p=128)[:, :, mc * 128:(mc + 1) * 128]
                                    u_src = up[e].rearrange("(ci p) m -> p ci m", p=128)[:, :, mc * 128:(mc + 1) * 128]
                                else:
                                    g_src = sg.rearrange("(ci p) m -> p ci m", p=128)[:, :, mc * 128:(mc + 1) * 128]
                                    u_src = su.rearrange("(ci p) m -> p ci m", p=128)[:, :, mc * 128:(mc + 1) * 128]
                                return [en.dma_start(out=wgap.rearrange("p (ci m) -> p ci m", ci=8), in_=g_src),
                                        en.dma_start(out=wuap.rearrange("p (ci m) -> p ci m", ci=8), in_=u_src)]
                            fwl.n_dma = 2
                            kb.dma_task("ge", fwl, writes=(f"f0wg{sl}",), tag=f"f0_wl{e}_{tqg}_{mc}")
                            psg, psgk = pb[sl], f"pb{sl}"
                            psu, psuk = pb[2 + sl], f"pb{2 + sl}"
                            def fmg(en, tqg=tqg, mc=mc, wgap=wgap, psg=psg):
                                last = None
                                for ci in range(8):
                                    last = en.matmul(psg[:, 0:512],
                                                     wgap[:, ci * 128:(ci + 1) * 128],
                                                     xnfm[:, ci * T + tqg * 512: ci * T + tqg * 512 + 512],
                                                     start=(ci == 0), stop=(ci == 7))
                                return last
                            rd = (f"f0wg{sl}",) + tuple(k for ci in range(8) for k in self.xnk(ci, tqg * 512, 512))
                            kb.task("te", fmg, reads=rd, writes=(psgk,), tag=f"f0_gm{e}_{tqg}_{mc}")
                            def fmu(en, tqg=tqg, mc=mc, wuap=wuap, psu=psu):
                                last = None
                                for ci in range(8):
                                    last = en.matmul(psu[:, 0:512],
                                                     wuap[:, ci * 128:(ci + 1) * 128],
                                                     xnfm[:, ci * T + tqg * 512: ci * T + tqg * 512 + 512],
                                                     start=(ci == 0), stop=(ci == 7))
                                return last
                            kb.task("te", fmu, reads=rd, writes=(psuk,), tag=f"f0_um{e}_{tqg}_{mc}")
                            def fsl(en, psg=psg):
                                return en.activation(out=sil[:, 0:512], in_=psg[:, 0:512],
                                                     func=AF.Silu)
                            kb.task("se", fsl, reads=(psgk,), writes=("f0sil",), tag=f"f0_sl{e}_{tqg}_{mc}")
                            def fhm(en, mc=mc, psu=psu):
                                return en.tensor_tensor(out=hfm[:, mc * 512:(mc + 1) * 512],
                                                        in0=sil[:, 0:512], in1=psu[:, 0:512],
                                                        op=ALU.mult)
                            kb.task("ve", fhm, reads=("f0sil", psuk), writes=(f"f0h{mc}",),
                                    tag=f"f0_hm{e}_{tqg}_{mc}")
                        for hh in range(2):
                            for mc in range(16):
                                sl = 0
                                wdap = wd[:, 0:512]
                                def fwd(en, e=e, mc=mc, hh=hh, wdap=wdap):
                                    if e < 8:
                                        src = dp[e, mc * 128:(mc + 1) * 128, hh * 512:(hh + 1) * 512]
                                    else:
                                        src = sd[mc * 128:(mc + 1) * 128, hh * 512:(hh + 1) * 512]
                                    return [en.dma_start(out=wdap, in_=src)]
                                kb.dma_task("ge", fwd, writes=(f"f0wd{sl}",), tag=f"f0_wd{e}_{tqg}_{hh}_{mc}")
                                def fdm(en, mc=mc, hh=hh, wdap=wdap):
                                    last = None
                                    for tt in range(4):
                                        last = en.matmul(pb[tt][:, 0:512],
                                                         hfm[:, mc * 512 + tt * 128: mc * 512 + tt * 128 + 128],
                                                         wdap[:, 0:512],
                                                         start=(mc == 0), stop=(mc == 15))
                                    return last
                                kb.task("te", fdm, reads=(f"f0wd{sl}", f"f0h{mc}"),
                                        writes=("pb0", "pb1", "pb2", "pb3"),
                                        tag=f"f0_dm{e}_{tqg}_{hh}_{mc}")
                            for tt in range(4):
                                tg = tqg * 4 + tt
                                ps = pb[tt]
                                cslice = comb[:, (tq2 * 4 + tt) * 1024 + hh * 512:
                                              (tq2 * 4 + tt) * 1024 + hh * 512 + 512]
                                ck = f"f0comb{tq2*4+tt}"
                                if e < 8:
                                    def fcb(en, tt=tt, tg=tg, e=e, ps=ps, cslice=cslice):
                                        return en.scalar_tensor_tensor(
                                            out=cslice, in0=ps[:, 0:512],
                                            scalar=cmat[:, tg * 8 + e: tg * 8 + e + 1],
                                            in1=cslice, op0=ALU.mult, op1=ALU.add)
                                    kb.task("ve", fcb, reads=(f"pb{tt}", f"f0cm{tg}", ck),
                                            writes=(ck,), tag=f"f0_cb{e}_{tqg}_{hh}_{tt}")
                                else:
                                    def fcb(en, tt=tt, ps=ps, cslice=cslice):
                                        return en.tensor_tensor(
                                            out=cslice, in0=ps[:, 0:512],
                                            in1=cslice, op=ALU.add)
                                    kb.task("ve", fcb, reads=(f"pb{tt}", ck),
                                            writes=(ck,), tag=f"f0_cb8_{tqg}_{hh}_{tt}")
                for tt in range(8):
                    tg = th * 8 + tt
                    def fsg(en, tt=tt):
                        return en.activation(out=self.sb["lnscr"][:, 0:1024],
                                             in_=comb[:, tt * 1024:(tt + 1) * 1024],
                                             func=AF.Silu)
                    kb.task("se", fsg, reads=(f"f0comb{th}_{tt // 1}",) if False else (f"f0comb{th}_{tt}",),
                            writes=("lnscr",), tag=f"f0_xg{th}_{tt}")
                    def fmul(en, tg=tg):
                        return en.tensor_tensor(out=self.sb["lnscr"][:, 0:1024],
                                                in0=self.sb["lnscr"][:, 0:1024],
                                                in1=br[:, tg * D:(tg + 1) * D], op=ALU.mult)
                    kb.task("ve", fmul, reads=("lnscr", f"br{tg}"), writes=("lnscr",),
                            tag=f"f0_xm{th}_{tt}")
                    def fadd(en, tg=tg):
                        return en.tensor_tensor(out=x[:, tg * D:(tg + 1) * D],
                                                in0=x[:, tg * D:(tg + 1) * D],
                                                in1=self.sb["lnscr"][:, 0:1024], op=ALU.add)
                    kb.task("ve", fadd, reads=("lnscr", self.xkey(tg)),
                            writes=(self.xkey(tg),), tag=f"f0_xa{th}_{tt}")



    def emit_l1_attn(self, seq):
        import contextlib
        nc, kb = self.nc, self.kb
        xnfm = self.sb["xnfm"]
        rows = self.sb["rows"]
        br = self.sb["br"]
        onesr = self.sb["ones_bf"]
        onesc = self.sb["ones_col"]
        hh2 = self.sb["hh2"]
        pb = [self.psum[f"pb{i}"] for i in range(6)]

        self.kb.fence()
        with contextlib.ExitStack() as es:
            KT = es.enter_context(nc.sbuf_tensor(f"l1K_{seq}", [128, NC * T], BF16))
            qstg = es.enter_context(nc.sbuf_tensor(f"l1qs_{seq}", [128, 8 * 512], BF16))
            expT = es.enter_context(nc.sbuf_tensor(f"l1e_{seq}", [128, 2 * 512], BF16))
            scr = es.enter_context(nc.sbuf_tensor(f"l1s_{seq}", [128, 512], BF16))
            biasc = es.enter_context(nc.sbuf_tensor(f"l1b_{seq}", [128, 32], F32))
            nrm = es.enter_context(nc.sbuf_tensor(f"l1n_{seq}", [2, 80], F32))
            negbc = es.enter_context(nc.sbuf_tensor(f"l1nb_{seq}", [128, 2], F32))
            zb = es.enter_context(nc.sbuf_tensor(f"l1zb_{seq}", [128, 512], F32))
            kwbuf = es.enter_context(nc.sbuf_tensor(f"l1kw_{seq}", [128, 2048], BF16))
            vwbuf = es.enter_context(nc.sbuf_tensor(f"l1vw_{seq}", [128, 4096], BF16))
            bvbo = es.enter_context(nc.sbuf_tensor(f"l1bv_{seq}", [1, 1024], BF16))

            self.emit_ln(f"l1ln_{seq}")

            def fb(en):
                return [en.dma_start(out=biasc[:, 0:8], in_=self.dram["L1_bq_c"]),
                        en.dma_start(out=biasc[:, 8:16], in_=self.dram["L1_bk_c"])]
            fb.n_dma = 2
            kb.dma_task("ge", fb, writes=("l1biasc",), tag="l1_biasload")
            def fbs(en):
                return en.tensor_scalar(out=biasc[:, 16:24], in0=biasc[:, 0:8],
                                        scalar1=0.125, scalar2=None, op0=ALU.mult)
            kb.task("ve", fbs, reads=("l1biasc",), writes=("l1biasq",), tag="l1_bscale")
            def fbv(en):
                return [en.dma_start(out=bvbo[0:1, 0:1024], in_=self.dram["L1_bv"])]
            kb.dma_task("ge", fbv, writes=("l1bvbo",), tag="l1_bvload")

            wq = self.dram["L1_wqkv"].rearrange("(ci p) o -> p ci o", p=128)
            for g in range(8):
                sl = g % 2
                wap = kwbuf[:, sl * 1024: sl * 1024 + 1024]
                wkey = f"l1kw{sl}"
                def fw(en, g=g, wap=wap):
                    return [en.dma_start(
                        out=wap.rearrange("p (ci o) -> p ci o", ci=8),
                        in_=wq[:, :, 1024 + g * 128: 1024 + g * 128 + 128])]
                kb.dma_task("ge", fw, writes=(wkey,), tag=f"l1_wk{g}")
                for tc in range(4):
                    ps, psk = pb[tc % 2], f"pb{tc % 2}"
                    def fm(en, tc=tc, ps=ps, wap=wap):
                        last = None
                        for ci in range(8):
                            last = en.matmul(ps[:, 0:512],
                                             wap[:, ci * 128:(ci + 1) * 128],
                                             xnfm[:, ci * T + tc * 512: ci * T + tc * 512 + 512],
                                             start=(ci == 0), stop=(ci == 7))
                        return last
                    rd = (wkey,) + tuple(k for ci in range(8) for k in self.xnk(ci, tc * 512, 512))
                    kb.task("te", fm, reads=rd, writes=(psk,), tag=f"l1_km{g}_{tc}")
                    def fe(en, g=g, tc=tc, ps=ps):
                        return en.activation(out=KT[:, g * T + tc * 512: g * T + tc * 512 + 512],
                                             in_=ps[:, 0:512], func=AF.Identity,
                                             bias=biasc[:, 8 + g: 9 + g], scale=1.0)
                    kb.task("se", fe, reads=(psk, "l1biasc"), writes=(f"K{g}",),
                            tag=f"l1_ke{g}_{tc}")

            for dblk in range(2):
                def fw(en, dblk=dblk):
                    return [en.dma_start(out=vwbuf[:].rearrange("p (ci o) -> p ci o", ci=8),
                                         in_=wq[:, :, 2048 + dblk * 512: 2048 + dblk * 512 + 512])]
                kb.dma_task("ge", fw, writes=("l1vw",), tag=f"l1_wv{dblk}")
                for t in range(NT):
                    ps, psk = pb[2 + t % 2], f"pb{2 + t % 2}"
                    def fm(en, dblk=dblk, t=t, ps=ps):
                        en.matmul(ps[:, 0:512], onesr[0:1, 0:128],
                                  bvbo[0:1, dblk * 512:(dblk + 1) * 512],
                                  start=True, stop=False)
                        last = None
                        for ci in range(8):
                            last = en.matmul(ps[:, 0:512],
                                             xnfm[:, ci * T + t * 128: ci * T + t * 128 + 128],
                                             vwbuf[:, ci * 512:(ci + 1) * 512],
                                             start=False, stop=(ci == 7))
                        return last
                    rd = ("l1vw", "l1bvbo") + tuple(k for ci in range(8) for k in self.xnk(ci, t * 128, 128))
                    kb.task("te", fm, reads=rd, writes=(psk,), tag=f"l1_vm{dblk}_{t}")
                    def fe(en, dblk=dblk, t=t, ps=ps):
                        return en.tensor_copy(
                            out=br[:, t * D + dblk * 512: t * D + dblk * 512 + 512],
                            in_=ps[:, 0:512])
                    kb.task("ve", fe, reads=(psk,), writes=(f"br{t}",),
                            tag=f"l1_ve{dblk}_{t}")

            for tc in range(4):
                for g in range(8):
                    sl = g % 2
                    wap = kwbuf[:, sl * 1024: sl * 1024 + 1024]
                    wkey = f"l1kw{sl}"
                    def fw(en, g=g, wap=wap):
                        return [en.dma_start(
                            out=wap.rearrange("p (ci o) -> p ci o", ci=8),
                            in_=wq[:, :, g * 128: g * 128 + 128])]
                    kb.dma_task("ge", fw, writes=(wkey,), tag=f"l1_wq{tc}_{g}")
                    ps, psk = pb[g % 2], f"pb{g % 2}"
                    def fm(en, tc=tc, ps=ps, wap=wap):
                        last = None
                        for ci in range(8):
                            last = en.matmul(ps[:, 0:512],
                                             wap[:, ci * 128:(ci + 1) * 128],
                                             xnfm[:, ci * T + tc * 512: ci * T + tc * 512 + 512],
                                             start=(ci == 0), stop=(ci == 7))
                        return last
                    rd = (wkey,) + tuple(k for ci in range(8) for k in self.xnk(ci, tc * 512, 512))
                    kb.task("te", fm, reads=rd, writes=(psk,), tag=f"l1_qm{tc}_{g}")
                    def fe(en, g=g, ps=ps):
                        return en.activation(out=qstg[:, g * 512: g * 512 + 512],
                                             in_=ps[:, 0:512], func=AF.Identity,
                                             bias=biasc[:, 16 + g: 17 + g],
                                             scale=self.cs(0.125))
                    kb.task("se", fe, reads=(psk, "l1biasq"), writes=(f"qstg{g}",),
                            tag=f"l1_qe{tc}_{g}")
                def fcp(en, tc=tc):
                    full = xnfm[:]
                    out = bass.AP(full.tensor, tc * 512, [full.ap[0], [T, NC], [1, 512]])
                    return en.tensor_copy(out=out, in_=qstg[:, 0:8 * 512])
                kb.task("ve", fcp,
                        reads=tuple(f"qstg{g}" for g in range(8)),
                        writes=tuple(f"xnfm{c}b{2 * tc + i}" for c in range(NC) for i in range(2)),
                        tag=f"l1_qcp{tc}")

            for part, base in ((0, 0), (1, 32)):
                for g in range(8):
                    for tc in range(4):
                        def fsq(en, part=part, g=g, tc=tc):
                            srcten = xnfm if part == 0 else KT
                            return en.activation(out=scr[:, 0:512],
                                                 in_=srcten[:, g * T + tc * 512: g * T + tc * 512 + 512],
                                                 func=AF.Square)
                        kb.task("se", fsq,
                                reads=((f"xnfm{g}b{2*tc}", f"xnfm{g}b{2*tc+1}") if part == 0 else (f"K{g}",)),
                                writes=("l1scr",), tag=f"l1_nsq{part}_{g}_{tc}")
                        def fm(en):
                            return en.matmul(pb[4][0:2, 0:512], hh2[:, 0:2],
                                             scr[:, 0:512], start=True, stop=True)
                        kb.task("te", fm, reads=("l1scr",), writes=("pb4",),
                                tag=f"l1_nm{part}_{g}_{tc}")
                        def fr(en, part=part, g=g, tc=tc, base=base):
                            return en.reduce_max(nrm[0:2, base + g * 4 + tc: base + g * 4 + tc + 1],
                                                 pb[4][0:2, 0:512], axis=AX.X)
                        kb.task("ve", fr, reads=("pb4",), writes=("l1nrm",),
                                tag=f"l1_nr{part}_{g}_{tc}")
            def fn1(en):
                return en.reduce_max(nrm[0:2, 64:65], nrm[0:2, 0:32], axis=AX.X)
            kb.task("ve", fn1, reads=("l1nrm",), writes=("l1nq",), tag="l1_nq")
            def fn2(en):
                return en.reduce_max(nrm[0:2, 65:66], nrm[0:2, 32:64], axis=AX.X)
            kb.task("ve", fn2, reads=("l1nrm",), writes=("l1nk",), tag="l1_nk")
            def fn3(en):
                return en.tensor_tensor(out=nrm[0:2, 66:67], in0=nrm[0:2, 64:65],
                                        in1=nrm[0:2, 65:66], op=ALU.mult)
            kb.task("ve", fn3, reads=("l1nq", "l1nk"), writes=("l1b2",), tag="l1_b2")
            def fn4(en):
                return en.activation(out=nrm[0:2, 67:68], in_=nrm[0:2, 66:67], func=AF.Sqrt)
            kb.task("se", fn4, reads=("l1b2",), writes=("l1b3",), tag="l1_b3")
            def fn5(en):
                return en.tensor_scalar(out=nrm[0:2, 68:69], in0=nrm[0:2, 67:68],
                                        scalar1=-1.0, scalar2=None, op0=ALU.mult)
            kb.task("ve", fn5, reads=("l1b3",), writes=("l1b4",), tag="l1_b4")
            def fn6a(en):
                return en.transpose(pb[4][0:1, 0:2], nrm[0:2, 68:69],
                                    self.sb["ident_f32"][0:2, 0:2])
            kb.task("te", fn6a, reads=("l1b4",), writes=("pb4",), tag="l1_btr")
            def fn6b(en):
                return en.tensor_copy(out=rows[0:1, 521:523], in_=pb[4][0:1, 0:2])
            kb.task("ve", fn6b, reads=("pb4",), writes=("rows_bb",), tag="l1_bte")
            for hh in range(2):
                def fn6(en, hh=hh):
                    return en.matmul(pb[4][:, hh:hh + 1], self.sb["ones_f32"][0:1, 0:128],
                                     rows[0:1, 521 + hh: 522 + hh],
                                     start=(hh == 0), stop=(hh == 1))
                kb.task("te", fn6, reads=("rows_bb",), writes=("pb4",), tag=f"l1_bb{hh}")
            def fn7(en):
                return en.tensor_copy(out=negbc[:, 0:2], in_=pb[4][:, 0:2])
            kb.task("ve", fn7, reads=("pb4",), writes=("l1negb",), tag="l1_bbc")

            for h in range(16):
                g, hh = h // 2, h % 2
                po = hh * 64
                for b in range(4):
                    for j in range(NT):
                        ps, psk = pb[j % 2], f"pb{j % 2}"
                        def fm(en, g=g, po=po, j=j, b=b, ps=ps):
                            return en.matmul(
                                ps[:, 0:512],
                                KT[po:po + 64, g * T + j * 128: g * T + j * 128 + 128],
                                xnfm[po:po + 64, g * T + b * 512: g * T + b * 512 + 512],
                                start=True, stop=True)
                        kb.task("te", fm,
                                reads=(f"K{g}", f"xnfm{g}b{2*b}", f"xnfm{g}b{2*b+1}"),
                                writes=(psk,), tag=f"l1_sm{h}_{b}_{j}")
                        def fexp(en, j=j, hh=hh, ps=ps):
                            return en.activation(out=expT[:, (j % 2) * 512:(j % 2) * 512 + 512],
                                                 in_=ps[:, 0:512], func=AF.Exp,
                                                 bias=negbc[:, hh:hh + 1], scale=1.0)
                        kb.task("se", fexp, reads=(psk, "l1negb"), writes=(f"expT{j % 2}",),
                                tag=f"l1_ex{h}_{b}_{j}")
                        def fz(en, j=j, po=po):
                            return en.matmul(pb[4][0:1, 0:512], onesc[:, 0:1],
                                             expT[:, (j % 2) * 512:(j % 2) * 512 + 512],
                                             start=(j == 0), stop=(j == NT - 1))
                        kb.task("te", fz, reads=(f"expT{j % 2}",), writes=("pb4",),
                                tag=f"l1_zm{h}_{b}_{j}")
                        def fc(en, j=j, h=h, ps=ps):
                            return en.matmul(pb[5][0:64, 0:512],
                                             br[:, j * D + h * 64: j * D + h * 64 + 64],
                                             expT[:, (j % 2) * 512:(j % 2) * 512 + 512],
                                             start=(j == 0), stop=(j == NT - 1))
                        kb.task("te", fc, reads=(f"expT{j % 2}", f"br{j}"), writes=("pb5",),
                                tag=f"l1_cm{h}_{b}_{j}")
                    def fz1(en):
                        return en.tensor_copy(out=rows[0:1, 0:512], in_=pb[4][0:1, 0:512])
                    kb.task("ve", fz1, reads=("pb4",), writes=("rows_z",), tag=f"l1_ze{h}_{b}")
                    def fz2(en):
                        return en.reciprocal(rows[0:1, 0:512], rows[0:1, 0:512])
                    kb.task("ve", fz2, reads=("rows_z",), writes=("rows_z",), tag=f"l1_zr{h}_{b}")
                    def fz3(en):
                        return en.matmul(pb[4][:, 0:512], self.sb["ones_f32"][0:1, 0:128],
                                         rows[0:1, 0:512], start=True, stop=True)
                    kb.task("te", fz3, reads=("rows_z",), writes=("pb4",), tag=f"l1_zb{h}_{b}")
                    def fz4(en):
                        return en.tensor_copy(out=zb[:, 0:512], in_=pb[4][:, 0:512])
                    kb.task("ve", fz4, reads=("pb4",), writes=("l1zb",), tag=f"l1_zc{h}_{b}")
                    def fce(en, g=g, po=po, b=b):
                        return en.tensor_tensor(
                            out=xnfm[po:po + 64, g * T + b * 512: g * T + b * 512 + 512],
                            in0=pb[5][0:64, 0:512], in1=zb[po:po + 64, 0:512], op=ALU.mult)
                    kb.task("ve", fce, reads=("pb5", "l1zb"),
                            writes=(f"xnfm{g}b{2*b}", f"xnfm{g}b{2*b+1}"),
                            tag=f"l1_ce{h}_{b}")

        self.emit_outproj(seq, "L1_wproj", "L1_bproj", "l1o")

    def emit_l1_ffn(self, seq):
        import contextlib
        nc, kb = self.nc, self.kb
        xnfm = self.sb["xnfm"]
        x = self.sb["x"]
        br = self.sb["br"]
        pb = [self.psum[f"pb{i}"] for i in range(6)]

        self.kb.fence()
        with contextlib.ExitStack() as es:
            ar1 = es.enter_context(nc.sbuf_tensor(f"f1a_{seq}", [128, NC * T], BF16))
            ar2 = es.enter_context(nc.sbuf_tensor(f"f1k_{seq}", [128, NC * T], BF16))
            mu = es.enter_context(nc.sbuf_tensor(f"f1mu_{seq}", [128, 32], F32))
            wcol = es.enter_context(nc.sbuf_tensor(f"f1w_{seq}", [128, 1024], BF16))
            sil = self.sb["lnscr"]
            tscr = es.enter_context(nc.sbuf_tensor(f"f1t_{seq}", [128, 512], BF16))

            self.emit_ln(f"f1ln_{seq}")
            def fmu(en):
                return [en.dma_start(out=mu[:, 0:8], in_=self.dram["L1_mur"]),
                        en.dma_start(out=mu[:, 8:16], in_=self.dram["L1_muk"])]
            fmu.n_dma = 2
            kb.dma_task("ge", fmu, writes=("f1mu",), tag="f1_mu")
            def fmu2(en):
                return en.tensor_scalar(out=mu[:, 16:32], in0=mu[:, 0:16],
                                        scalar1=-1.0, scalar2=1.0, op0=ALU.mult, op1=ALU.add)
            kb.task("ve", fmu2, reads=("f1mu",), writes=("f1mu1",), tag="f1_mu1")

            def mix(dst, dkey, mucol, tag2):
                for c in range(NC):
                    def g1(en, c=c, mucol=mucol):
                        return en.tensor_scalar(
                            out=dst[:, c * T: c * T + T],
                            in0=xnfm[:, c * T: c * T + T],
                            scalar1=mu[:, mucol + c: mucol + c + 1], scalar2=None,
                            op0=ALU.mult)
                    kb.task("ve", g1,
                            reads=("f1mu", "f1mu1") + tuple(f"xnfm{c}b{i}" for i in range(8)),
                            writes=(f"{dkey}{c}",), tag=f"{tag2}a{c}")
                    def g2(en, c=c, mucol=mucol):
                        return en.scalar_tensor_tensor(
                            out=dst[:, c * T + 1: c * T + T],
                            in0=xnfm[:, c * T: c * T + T - 1],
                            scalar=mu[:, 16 + mucol + c: 16 + mucol + c + 1],
                            in1=dst[:, c * T + 1: c * T + T],
                            op0=ALU.mult, op1=ALU.add)
                    kb.task("ve", g2,
                            reads=("f1mu1", f"{dkey}{c}") + tuple(f"xnfm{c}b{i}" for i in range(8)),
                            writes=(f"{dkey}{c}",), tag=f"{tag2}b{c}")

            mix(ar1, "f1x", 8, f"f1xk_{seq}")
            wk = self.dram["L1_WkT"].rearrange("(ci p) o -> p ci o", p=128)
            for dc in range(8):
                wap = wcol[:, 0:1024]
                wkey = "f1w0"
                def fw(en, dc=dc, wap=wap):
                    return [en.dma_start(out=wap.rearrange("p (ci o) -> p ci o", ci=8),
                                         in_=wk[:, :, dc * 128:(dc + 1) * 128])]
                kb.dma_task("ge", fw, writes=(wkey,), tag=f"f1_wk{dc}")
                for tc in range(4):
                    ps, psk = pb[tc % 2], f"pb{tc % 2}"
                    def fm(en, tc=tc, ps=ps, wap=wap):
                        last = None
                        for ci in range(8):
                            last = en.matmul(ps[:, 0:512],
                                             wap[:, ci * 128:(ci + 1) * 128],
                                             ar1[:, ci * T + tc * 512: ci * T + tc * 512 + 512],
                                             start=(ci == 0), stop=(ci == 7))
                        return last
                    kb.task("te", fm, reads=(wkey,) + tuple(f"f1x{ci}" for ci in range(8)),
                            writes=(psk,), tag=f"f1_km{dc}_{tc}")
                    def fr1(en, ps=ps):
                        return en.tensor_scalar(out=tscr[:, 0:512], in0=ps[:, 0:512],
                                                scalar1=0.0, scalar2=None, op0=ALU.max)
                    kb.task("ve", fr1, reads=(psk,), writes=("f1tscr",), tag=f"f1_kr{dc}_{tc}")
                    def fr2(en, dc=dc, tc=tc):
                        return en.activation(out=ar2[:, dc * T + tc * 512: dc * T + tc * 512 + 512],
                                             in_=tscr[:, 0:512], func=AF.Square)
                    kb.task("se", fr2, reads=("f1tscr",), writes=(f"f1k{dc}",),
                            tag=f"f1_ks{dc}_{tc}")
            mix(ar1, "f1x", 0, f"f1xr_{seq}")
            wr = self.dram["L1_WrT"].rearrange("(ci p) o -> p ci o", p=128)
            for oc in range(8):
                wap = wcol[:, 0:1024]
                wkey = "f1w0"
                def fw(en, oc=oc, wap=wap):
                    return [en.dma_start(out=wap.rearrange("p (ci o) -> p ci o", ci=8),
                                         in_=wr[:, :, oc * 128:(oc + 1) * 128])]
                kb.dma_task("ge", fw, writes=(wkey,), tag=f"f1_wr{oc}")
                for t in range(NT):
                    ps, psk = pb[2 + t % 2], f"pb{2 + t % 2}"
                    def fm(en, t=t, ps=ps, wap=wap):
                        last = None
                        for ci in range(8):
                            last = en.matmul(ps[:, 0:128],
                                             ar1[:, ci * T + t * 128: ci * T + t * 128 + 128],
                                             wap[:, ci * 128:(ci + 1) * 128],
                                             start=(ci == 0), stop=(ci == 7))
                        return last
                    kb.task("te", fm, reads=(wkey,) + tuple(f"f1x{ci}" for ci in range(8)),
                            writes=(psk,), tag=f"f1_rm{t}_{oc}")
                    def fe(en, t=t, oc=oc, ps=ps):
                        return en.activation(out=xnfm[:, t * D + oc * 128: t * D + oc * 128 + 128],
                                             in_=ps[:, 0:128], func=AF.Sigmoid)
                    kb.task("se", fe, reads=(psk,),
                            writes=(f"f1r{t}", f"xnfm{t // 2}b{(t % 2) * 4 + oc // 2}"),
                            tag=f"f1_re{t}_{oc}")
            wv = self.dram["L1_WvT"].rearrange("(ci p) o -> p ci o", p=128)
            for oc in range(8):
                wap = wcol[:, 0:1024]
                wkey = "f1w0"
                def fw(en, oc=oc, wap=wap):
                    return [en.dma_start(out=wap.rearrange("p (ci o) -> p ci o", ci=8),
                                         in_=wv[:, :, oc * 128:(oc + 1) * 128])]
                kb.dma_task("ge", fw, writes=(wkey,), tag=f"f1_wv{oc}")
                for t in range(NT):
                    ps, psk = pb[4 + t % 2], f"pb{4 + t % 2}"
                    def fm(en, t=t, ps=ps, wap=wap):
                        last = None
                        for ci in range(8):
                            last = en.matmul(ps[:, 0:128],
                                             ar2[:, ci * T + t * 128: ci * T + t * 128 + 128],
                                             wap[:, ci * 128:(ci + 1) * 128],
                                             start=(ci == 0), stop=(ci == 7))
                        return last
                    kb.task("te", fm, reads=(wkey,) + tuple(f"f1k{ci}" for ci in range(8)),
                            writes=(psk,), tag=f"f1_om{t}_{oc}")
                    def f1(en, t=t, oc=oc, ps=ps):
                        return en.tensor_tensor(out=tscr[:, 0:128], in0=ps[:, 0:128],
                                                in1=xnfm[:, t * D + oc * 128: t * D + oc * 128 + 128],
                                                op=ALU.mult)
                    kb.task("ve", f1, reads=(psk, f"f1r{t}"), writes=("f1tscr",),
                            tag=f"f1_g1{t}_{oc}")
                    def f2(en):
                        return en.activation(out=sil[:, 0:128], in_=tscr[:, 0:128], func=AF.Silu)
                    kb.task("se", f2, reads=("f1tscr",), writes=("lnscr",), tag=f"f1_g2{t}_{oc}")
                    def f3(en, t=t, oc=oc):
                        return en.tensor_tensor(out=sil[:, 0:128], in0=sil[:, 0:128],
                                                in1=br[:, t * D + oc * 128: t * D + oc * 128 + 128],
                                                op=ALU.mult)
                    kb.task("ve", f3, reads=("lnscr", f"br{t}"), writes=("lnscr",),
                            tag=f"f1_g3{t}_{oc}")
                    def f4(en, t=t, oc=oc):
                        return en.tensor_tensor(out=x[:, t * D + oc * 128: t * D + oc * 128 + 128],
                                                in0=x[:, t * D + oc * 128: t * D + oc * 128 + 128],
                                                in1=sil[:, 0:128], op=ALU.add)
                    kb.task("ve", f4, reads=("lnscr", self.xkey(t)), writes=(self.xkey(t),),
                            tag=f"f1_g4{t}_{oc}")

    def emit_l2_attn(self, seq):
        import contextlib
        nc, kb = self.nc, self.kb
        xnfm = self.sb["xnfm"]
        rows = self.sb["rows"]
        br = self.sb["br"]
        onesc = self.sb["ones_col"]
        hh2 = self.sb["hh2"]
        pb = [self.psum[f"pb{i}"] for i in range(6)]
        SEG = 512

        self.kb.fence()
        with contextlib.ExitStack() as es:
            KT = es.enter_context(nc.sbuf_tensor(f"l2K_{seq}", [128, NC * T], BF16))
            qstg = es.enter_context(nc.sbuf_tensor(f"l2qs_{seq}", [128, 8 * 512], BF16))
            expT = es.enter_context(nc.sbuf_tensor(f"l2e_{seq}", [128, 2 * 512], BF16))
            expP = es.enter_context(nc.sbuf_tensor(f"l2ep_{seq}", [16, 512], BF16))
            scr = es.enter_context(nc.sbuf_tensor(f"l2s_{seq}", [128, 512], BF16))
            l2st = es.enter_context(nc.sbuf_tensor(f"l2t_{seq}", [128, 48], F32))
            nrm = es.enter_context(nc.sbuf_tensor(f"l2n_{seq}", [2, 80], F32))
            negbc = es.enter_context(nc.sbuf_tensor(f"l2nb_{seq}", [128, 2], F32))
            zb = es.enter_context(nc.sbuf_tensor(f"l2zb_{seq}", [128, 512], F32))
            kwbuf = es.enter_context(nc.sbuf_tensor(f"l2kw_{seq}", [128, 2048], BF16))
            vwbuf = es.enter_context(nc.sbuf_tensor(f"l2vw_{seq}", [128, 2048], BF16))
            pmk = es.enter_context(nc.sbuf_tensor(f"l2pk_{seq}", [128, 128], BF16))
            pmv = es.enter_context(nc.sbuf_tensor(f"l2pv_{seq}", [16, 1024], BF16))
            ropc = es.enter_context(nc.sbuf_tensor(f"l2rc_{seq}", [128, 2048], BF16))
            rops = es.enter_context(nc.sbuf_tensor(f"l2rs_{seq}", [128, 2048], BF16))
            rotm = es.enter_context(nc.sbuf_tensor(f"l2ro_{seq}", [128, 128], BF16))
            nw = es.enter_context(nc.sbuf_tensor(f"l2nw_{seq}", [128, 8], F32))

            def post(kb2, stats):
                def p1(en):
                    return en.reciprocal(l2st[:, 0:16], stats[:, 16:32])
                kb2.task("ve", p1, reads=("stV",), writes=("l2p1",), tag="l2_p1")
                def p2(en):
                    return en.tensor_scalar(out=l2st[:, 16:32], in0=l2st[:, 0:16],
                                            scalar1=-1e-5, scalar2=1.000001,
                                            op0=ALU.mult, op1=ALU.add)
                kb2.task("ve", p2, reads=("l2p1",), writes=("l2p2",), tag="l2_p2")
                def p3(en):
                    return en.activation(out=l2st[:, 32:48], in_=l2st[:, 16:32], func=AF.Sqrt)
                kb2.task("se", p3, reads=("l2p2",), writes=("l2p3",), tag="l2_p3")
                def p4(en):
                    return en.reciprocal(l2st[:, 32:48], l2st[:, 32:48])
                kb2.task("ve", p4, reads=("l2p3",), writes=("l2p4",), tag="l2_p4")
                def p5(en):
                    return en.activation(out=l2st[:, 0:16], in_=l2st[:, 32:48], func=AF.Square)
                kb2.task("se", p5, reads=("l2p4",), writes=("l2p5",), tag="l2_p5")
                def p6(en):
                    return en.tensor_tensor(out=l2st[:, 0:16], in0=l2st[:, 0:16],
                                            in1=l2st[:, 16:32], op=ALU.mult)
                kb2.task("ve", p6, reads=("l2p5", "l2p2"), writes=("l2p5",), tag="l2_p6")
                def p7(en):
                    return en.tensor_scalar(out=l2st[:, 0:16], in0=l2st[:, 0:16],
                                            scalar1=-0.5, scalar2=1.5, op0=ALU.mult, op1=ALU.add)
                kb2.task("ve", p7, reads=("l2p5",), writes=("l2p5",), tag="l2_p7")
                def p8(en):
                    return en.tensor_tensor(out=l2st[:, 32:48], in0=l2st[:, 32:48],
                                            in1=l2st[:, 0:16], op=ALU.mult)
                kb2.task("ve", p8, reads=("l2p5", "l2p4"), writes=("l2p8",), tag="l2_p8")
                def p9(en):
                    return en.tensor_tensor(out=stats[:, 32:48], in0=stats[:, 32:48],
                                            in1=l2st[:, 32:48], op=ALU.mult)
                kb2.task("ve", p9, reads=("l2p8", "stB"), writes=("stB",), tag="l2_p9")
            self.emit_ln(f"l2ln_{seq}", post_stats=post)

            def fnw(en):
                return [en.dma_start(out=nw[:, 0:8], in_=self.dram["L2_normw"])]
            kb.dma_task("ge", fnw, writes=("l2nw",), tag="l2_nw")
            for ci in range(8):
                def fsc(en, ci=ci):
                    return en.tensor_scalar(out=xnfm[:, ci * T:(ci + 1) * T],
                                            in0=xnfm[:, ci * T:(ci + 1) * T],
                                            scalar1=nw[:, ci:ci + 1], scalar2=None,
                                            op0=ALU.mult)
                kb.task("ve", fsc,
                        reads=("l2nw",) + tuple(f"xnfm{ci}b{i}" for i in range(8)),
                        writes=tuple(f"xnfm{ci}b{i}" for i in range(8)), tag=f"l2_nws{ci}")

            def frt(en):
                return [en.dma_start(out=ropc[:], in_=self.dram["c_rope_cos"]),
                        en.dma_start(out=rops[:], in_=self.dram["c_rope_sin"]),
                        en.dma_start(out=rotm[:], in_=self.dram["c_rot"])]
            frt.n_dma = 3
            kb.dma_task("ge", frt, writes=("l2rope",), tag="l2_rope")
            def fpm(en):
                return [en.dma_start(out=pmk[:], in_=self.dram["L2_pmkT"]),
                        en.dma_start(out=pmv[:], in_=self.dram["L2_pmv"])]
            fpm.n_dma = 2
            kb.dma_task("ge", fpm, writes=("l2pm",), tag="l2_pm")

            wq = self.dram["L2_wqkv"].rearrange("(ci p) o -> p ci o", p=128)

            def rope_tile(dten, coff, dkey, tc, tag2, psl):
                ps, psk = pb[psl], f"pb{psl}"
                def rmm(en, coff=coff):
                    return en.matmul(ps[:, 0:512], rotm[:],
                                     dten[:, coff:coff + 512], start=True, stop=True)
                kb.task("te", rmm, reads=(dkey, "l2rope"), writes=(psk,), tag=f"{tag2}mm")
                def ro1(en, coff=coff, tc=tc):
                    return en.tensor_tensor(out=dten[:, coff:coff + 512],
                                            in0=dten[:, coff:coff + 512],
                                            in1=ropc[:, tc * 512:(tc + 1) * 512],
                                            op=ALU.mult)
                kb.task("ve", ro1, reads=(dkey, "l2rope"), writes=(dkey,), tag=f"{tag2}o1")
                def ro2(en, tc=tc):
                    return en.tensor_tensor(out=scr[:, 0:512], in0=ps[:, 0:512],
                                            in1=rops[:, tc * 512:(tc + 1) * 512],
                                            op=ALU.mult)
                kb.task("ve", ro2, reads=(psk, "l2rope"), writes=("l2scr",), tag=f"{tag2}o2")
                def ro3(en, coff=coff):
                    return en.tensor_tensor(out=dten[:, coff:coff + 512],
                                            in0=dten[:, coff:coff + 512],
                                            in1=scr[:, 0:512], op=ALU.add)
                kb.task("ve", ro3, reads=(dkey, "l2scr"), writes=(dkey,), tag=f"{tag2}o3")

            for g in range(8):
                sl = g % 2
                wap = kwbuf[:, sl * 1024: sl * 1024 + 1024]
                wkey = f"l2kw{sl}"
                def fw(en, g=g, wap=wap):
                    return [en.dma_start(
                        out=wap.rearrange("p (ci o) -> p ci o", ci=8),
                        in_=wq[:, :, 1024 + g * 128: 1024 + g * 128 + 128])]
                kb.dma_task("ge", fw, writes=(wkey,), tag=f"l2_wk{g}")
                for tc in range(4):
                    ps, psk = pb[tc % 2], f"pb{tc % 2}"
                    def fm(en, tc=tc, ps=ps, wap=wap):
                        last = None
                        for ci in range(8):
                            last = en.matmul(ps[:, 0:512],
                                             wap[:, ci * 128:(ci + 1) * 128],
                                             xnfm[:, ci * T + tc * 512: ci * T + tc * 512 + 512],
                                             start=(ci == 0), stop=(ci == 7))
                        return last
                    rd = (wkey,) + tuple(k for ci in range(8) for k in self.xnk(ci, tc * 512, 512))
                    kb.task("te", fm, reads=rd, writes=(psk,), tag=f"l2_km{g}_{tc}")
                    def fe(en, g=g, tc=tc, ps=ps):
                        return en.tensor_copy(out=KT[:, g * T + tc * 512: g * T + tc * 512 + 512],
                                              in_=ps[:, 0:512])
                    kb.task("ve", fe, reads=(psk,), writes=(f"K{g}",), tag=f"l2_ke{g}_{tc}")
                    rope_tile(KT, g * T + tc * 512, f"K{g}", tc, f"l2_kr{g}_{tc}", 2 + tc % 2)

            for dblk in range(4):
                def fw(en, dblk=dblk):
                    return [en.dma_start(out=vwbuf[:].rearrange("p (ci o) -> p ci o", ci=8),
                                         in_=wq[:, :, 2048 + dblk * 256: 2048 + dblk * 256 + 256])]
                kb.dma_task("ge", fw, writes=("l2vw",), tag=f"l2_wv{dblk}")
                for t in range(NT):
                    ps, psk = pb[2 + t % 2], f"pb{2 + t % 2}"
                    def fm(en, dblk=dblk, t=t, ps=ps):
                        last = None
                        for ci in range(8):
                            last = en.matmul(ps[:, 0:256],
                                             xnfm[:, ci * T + t * 128: ci * T + t * 128 + 128],
                                             vwbuf[:, ci * 256:(ci + 1) * 256],
                                             start=(ci == 0), stop=(ci == 7))
                        return last
                    rd = ("l2vw",) + tuple(k for ci in range(8) for k in self.xnk(ci, t * 128, 128))
                    kb.task("te", fm, reads=rd, writes=(psk,), tag=f"l2_vm{dblk}_{t}")
                    def fe(en, dblk=dblk, t=t, ps=ps):
                        return en.tensor_copy(
                            out=br[:, t * D + dblk * 256: t * D + dblk * 256 + 256],
                            in_=ps[:, 0:256])
                    kb.task("ve", fe, reads=(psk,), writes=(f"br{t}",),
                            tag=f"l2_ve{dblk}_{t}")

            for tc in range(4):
                for g in range(8):
                    sl = g % 2
                    wap = kwbuf[:, sl * 1024: sl * 1024 + 1024]
                    wkey = f"l2kw{sl}"
                    def fw(en, g=g, wap=wap):
                        return [en.dma_start(
                            out=wap.rearrange("p (ci o) -> p ci o", ci=8),
                            in_=wq[:, :, g * 128: g * 128 + 128])]
                    kb.dma_task("ge", fw, writes=(wkey,), tag=f"l2_wq{tc}_{g}")
                    ps, psk = pb[g % 2], f"pb{g % 2}"
                    def fm(en, tc=tc, ps=ps, wap=wap):
                        last = None
                        for ci in range(8):
                            last = en.matmul(ps[:, 0:512],
                                             wap[:, ci * 128:(ci + 1) * 128],
                                             xnfm[:, ci * T + tc * 512: ci * T + tc * 512 + 512],
                                             start=(ci == 0), stop=(ci == 7))
                        return last
                    rd = (wkey,) + tuple(k for ci in range(8) for k in self.xnk(ci, tc * 512, 512))
                    kb.task("te", fm, reads=rd, writes=(psk,), tag=f"l2_qm{tc}_{g}")
                    def fe(en, g=g, ps=ps):
                        return en.activation(out=qstg[:, g * 512: g * 512 + 512],
                                             in_=ps[:, 0:512], func=AF.Identity,
                                             scale=self.cs(0.125))
                    kb.task("se", fe, reads=(psk,), writes=(f"qstg{g}",), tag=f"l2_qe{tc}_{g}")
                    rope_tile(qstg, g * 512, f"qstg{g}", tc, f"l2_qr{tc}_{g}", 2 + g % 2)
                def fcp(en, tc=tc):
                    full = xnfm[:]
                    out = bass.AP(full.tensor, tc * 512, [full.ap[0], [T, NC], [1, 512]])
                    return en.tensor_copy(out=out, in_=qstg[:, 0:8 * 512])
                kb.task("ve", fcp,
                        reads=tuple(f"qstg{g}" for g in range(8)),
                        writes=tuple(f"xnfm{c}b{2 * tc + i}" for c in range(NC) for i in range(2)),
                        tag=f"l2_qcp{tc}")

            for part, base in ((0, 0), (1, 32)):
                for g in range(8):
                    for tc in range(4):
                        def fsq(en, part=part, g=g, tc=tc):
                            srcten = xnfm if part == 0 else KT
                            return en.activation(out=scr[:, 0:512],
                                                 in_=srcten[:, g * T + tc * 512: g * T + tc * 512 + 512],
                                                 func=AF.Square)
                        kb.task("se", fsq,
                                reads=((f"xnfm{g}b{2*tc}", f"xnfm{g}b{2*tc+1}") if part == 0 else (f"K{g}",)),
                                writes=("l2scr",), tag=f"l2_nsq{part}_{g}_{tc}")
                        def fm(en):
                            return en.matmul(pb[4][0:2, 0:512], hh2[:, 0:2],
                                             scr[:, 0:512], start=True, stop=True)
                        kb.task("te", fm, reads=("l2scr",), writes=("pb4",),
                                tag=f"l2_nm{part}_{g}_{tc}")
                        def fr(en, part=part, g=g, tc=tc, base=base):
                            return en.reduce_max(nrm[0:2, base + g * 4 + tc: base + g * 4 + tc + 1],
                                                 pb[4][0:2, 0:512], axis=AX.X)
                        kb.task("ve", fr, reads=("pb4",), writes=("l2nrm",),
                                tag=f"l2_nr{part}_{g}_{tc}")
            def fpq(en):
                return en.activation(out=scr[:, 0:128], in_=pmk[:], func=AF.Square)
            kb.task("se", fpq, reads=("l2pm",), writes=("l2scr",), tag="l2_pmsq")
            def fpm2(en):
                return en.matmul(pb[4][0:2, 0:128], hh2[:, 0:2], scr[:, 0:128],
                                 start=True, stop=True)
            kb.task("te", fpm2, reads=("l2scr",), writes=("pb4",), tag="l2_pmn")
            def fpr(en):
                return en.reduce_max(nrm[0:2, 69:70], pb[4][0:2, 0:128], axis=AX.X)
            kb.task("ve", fpr, reads=("pb4",), writes=("l2pmn",), tag="l2_pmr")
            def fn1(en):
                return en.reduce_max(nrm[0:2, 64:65], nrm[0:2, 0:32], axis=AX.X)
            kb.task("ve", fn1, reads=("l2nrm",), writes=("l2nq",), tag="l2_nq")
            def fn2(en):
                return en.reduce_max(nrm[0:2, 65:66], nrm[0:2, 32:64], axis=AX.X)
            kb.task("ve", fn2, reads=("l2nrm",), writes=("l2nk",), tag="l2_nk")
            def fn2b(en):
                return en.tensor_tensor(out=nrm[0:2, 65:66], in0=nrm[0:2, 65:66],
                                        in1=nrm[0:2, 69:70], op=ALU.max)
            kb.task("ve", fn2b, reads=("l2nk", "l2pmn"), writes=("l2nk",), tag="l2_nk2")
            def fn3(en):
                return en.tensor_tensor(out=nrm[0:2, 66:67], in0=nrm[0:2, 64:65],
                                        in1=nrm[0:2, 65:66], op=ALU.mult)
            kb.task("ve", fn3, reads=("l2nq", "l2nk"), writes=("l2b2",), tag="l2_b2")
            def fn4(en):
                return en.activation(out=nrm[0:2, 67:68], in_=nrm[0:2, 66:67], func=AF.Sqrt)
            kb.task("se", fn4, reads=("l2b2",), writes=("l2b3",), tag="l2_b3")
            def fn5(en):
                return en.tensor_scalar(out=nrm[0:2, 68:69], in0=nrm[0:2, 67:68],
                                        scalar1=-1.0, scalar2=None, op0=ALU.mult)
            kb.task("ve", fn5, reads=("l2b3",), writes=("l2b4",), tag="l2_b4")
            def fn6a(en):
                return en.transpose(pb[4][0:1, 0:2], nrm[0:2, 68:69],
                                    self.sb["ident_f32"][0:2, 0:2])
            kb.task("te", fn6a, reads=("l2b4",), writes=("pb4",), tag="l2_btr")
            def fn6b(en):
                return en.tensor_copy(out=rows[0:1, 521:523], in_=pb[4][0:1, 0:2])
            kb.task("ve", fn6b, reads=("pb4",), writes=("rows_bb",), tag="l2_bte")
            for hh in range(2):
                def fn6(en, hh=hh):
                    return en.matmul(pb[4][:, hh:hh + 1], self.sb["ones_f32"][0:1, 0:128],
                                     rows[0:1, 521 + hh: 522 + hh],
                                     start=(hh == 0), stop=(hh == 1))
                kb.task("te", fn6, reads=("rows_bb",), writes=("pb4",), tag=f"l2_bb{hh}")
            def fn7(en):
                return en.tensor_copy(out=negbc[:, 0:2], in_=pb[4][:, 0:2])
            kb.task("ve", fn7, reads=("pb4",), writes=("l2negb",), tag="l2_bbc")

            for h in range(16):
                g, hh = h // 2, h % 2
                po = hh * 64
                for sg in range(4):
                    qlo = sg * SEG
                    def fmp(en, g=g, po=po, h=h, qlo=qlo):
                        return en.matmul(
                            pb[0][0:16, 0:512],
                            pmk[po:po + 64, g * 16:(g + 1) * 16],
                            xnfm[po:po + 64, g * T + qlo: g * T + qlo + 512],
                            start=True, stop=True)
                    kb.task("te", fmp, reads=("l2pm", f"xnfm{g}b{2*sg}", f"xnfm{g}b{2*sg+1}"),
                            writes=("pb0",), tag=f"l2_pm{h}_{sg}")
                    def fep(en, hh=hh):
                        return en.activation(out=expP[0:16, 0:512], in_=pb[0][0:16, 0:512],
                                             func=AF.Exp, bias=negbc[0:16, hh:hh + 1], scale=1.0)
                    kb.task("se", fep, reads=("pb0", "l2negb"), writes=("expP",),
                            tag=f"l2_pe{h}_{sg}")
                    def fzp(en):
                        return en.matmul(pb[4][0:1, 0:512], onesc[0:16, 0:1],
                                         expP[0:16, 0:512], start=True, stop=False)
                    kb.task("te", fzp, reads=("expP",), writes=("pb4",), tag=f"l2_zp{h}_{sg}")
                    def fcp2(en, h=h):
                        return en.matmul(pb[5][0:64, 0:512],
                                         pmv[0:16, h * 64: h * 64 + 64],
                                         expP[0:16, 0:512], start=True, stop=False)
                    kb.task("te", fcp2, reads=("expP", "l2pm"), writes=("pb5",),
                            tag=f"l2_cp{h}_{sg}")
                    for j in range(4):
                        kt = sg * 4 + j
                        ps, psk = pb[1 + j % 2], f"pb{1 + j % 2}"
                        def fm(en, g=g, po=po, kt=kt, qlo=qlo, ps=ps):
                            return en.matmul(
                                ps[:, 0:512],
                                KT[po:po + 64, g * T + kt * 128: g * T + kt * 128 + 128],
                                xnfm[po:po + 64, g * T + qlo: g * T + qlo + 512],
                                start=True, stop=True)
                        kb.task("te", fm,
                                reads=(f"K{g}", f"xnfm{g}b{2*sg}", f"xnfm{g}b{2*sg+1}"),
                                writes=(psk,), tag=f"l2_sm{h}_{sg}_{j}")
                        for c2 in range(4):
                            off = c2 - j
                            if off > 0:
                                continue
                            mask = self.sb["diagmask"] if off == 0 else self.sb["allmask"]
                            def fmask(en, c2=c2, ps=ps, mask=mask):
                                return en.tensor_tensor(out=ps[:, c2 * 128:(c2 + 1) * 128],
                                                        in0=ps[:, c2 * 128:(c2 + 1) * 128],
                                                        in1=mask[:], op=ALU.add)
                            kb.task("ve", fmask, reads=(psk,), writes=(psk,),
                                    tag=f"l2_msk{h}_{sg}_{j}_{c2}")
                        def fexp(en, j=j, hh=hh, ps=ps):
                            return en.activation(out=expT[:, (j % 2) * 512:(j % 2) * 512 + 512],
                                                 in_=ps[:, 0:512], func=AF.Exp,
                                                 bias=negbc[:, hh:hh + 1], scale=1.0)
                        kb.task("se", fexp, reads=(psk, "l2negb"), writes=(f"expT{j % 2}",),
                                tag=f"l2_ex{h}_{sg}_{j}")
                        def fz(en, j=j):
                            return en.matmul(pb[4][0:1, 0:512], onesc[:, 0:1],
                                             expT[:, (j % 2) * 512:(j % 2) * 512 + 512],
                                             start=False, stop=(j == 3))
                        kb.task("te", fz, reads=(f"expT{j % 2}",), writes=("pb4",),
                                tag=f"l2_zm{h}_{sg}_{j}")
                        def fc(en, j=j, kt=kt, h=h, ps=ps):
                            return en.matmul(pb[5][0:64, 0:512],
                                             br[:, kt * D + h * 64: kt * D + h * 64 + 64],
                                             expT[:, (j % 2) * 512:(j % 2) * 512 + 512],
                                             start=False, stop=(j == 3))
                        kb.task("te", fc, reads=(f"expT{j % 2}", f"br{kt}"), writes=("pb5",),
                                tag=f"l2_cm{h}_{sg}_{j}")
                    def fz1(en):
                        return en.tensor_copy(out=rows[0:1, 0:512], in_=pb[4][0:1, 0:512])
                    kb.task("ve", fz1, reads=("pb4",), writes=("rows_z",), tag=f"l2_ze{h}_{sg}")
                    def fz2(en):
                        return en.reciprocal(rows[0:1, 0:512], rows[0:1, 0:512])
                    kb.task("ve", fz2, reads=("rows_z",), writes=("rows_z",), tag=f"l2_zr{h}_{sg}")
                    def fz3(en):
                        return en.matmul(pb[4][:, 0:512], self.sb["ones_f32"][0:1, 0:128],
                                         rows[0:1, 0:512], start=True, stop=True)
                    kb.task("te", fz3, reads=("rows_z",), writes=("pb4",), tag=f"l2_zb{h}_{sg}")
                    def fz4(en):
                        return en.tensor_copy(out=zb[:, 0:512], in_=pb[4][:, 0:512])
                    kb.task("ve", fz4, reads=("pb4",), writes=("l2zb",), tag=f"l2_zc{h}_{sg}")
                    def fce(en, g=g, po=po, qlo=qlo):
                        return en.tensor_tensor(
                            out=xnfm[po:po + 64, g * T + qlo: g * T + qlo + 512],
                            in0=pb[5][0:64, 0:512], in1=zb[po:po + 64, 0:512], op=ALU.mult)
                    kb.task("ve", fce, reads=("pb5", "l2zb"),
                            writes=(f"xnfm{g}b{2*sg}", f"xnfm{g}b{2*sg+1}"),
                            tag=f"l2_ce{h}_{sg}")

        self.emit_outproj(seq, "L2_wout", "L2_bout", "l2o")

    def emit_l2_ffn(self, seq):
        import contextlib
        nc, kb = self.nc, self.kb
        xnfm = self.sb["xnfm"]
        x = self.sb["x"]
        br = self.sb["br"]
        pb = [self.psum[f"pb{i}"] for i in range(6)]

        self.kb.fence()
        with contextlib.ExitStack() as es:
            scomb = es.enter_context(nc.sbuf_tensor(f"f2c_{seq}", [128, 32 * 512], BF16))
            Bsel = es.enter_context(nc.sbuf_tensor(f"f2B_{seq}", [16, 2 * 4096], BF16))
            Ble = es.enter_context(nc.sbuf_tensor(f"f2Bl_{seq}", [16, 2048], BF16))
            Asel = es.enter_context(nc.sbuf_tensor(f"f2A_{seq}", [128, 2 * 128], BF16))
            l1f = es.enter_context(nc.sbuf_tensor(f"f2l1_{seq}", [16, 2 * 2048], BF16))
            mb = es.enter_context(nc.sbuf_tensor(f"f2mb_{seq}", [128, 20], F32))
            wbuf = es.enter_context(nc.sbuf_tensor(f"f2w_{seq}", [128, 3200], BF16))
            tscr = es.enter_context(nc.sbuf_tensor(f"f2t_{seq}", [128, 512], BF16))
            sil = self.sb["lnscr"]

            self.emit_ln(f"f2ln_{seq}")

            def fidx(en):
                return [en.dma_start(out=mb[0:1, 16:18], in_=self.dram["L2_topk_i"]),
                        en.dma_start(out=mb[0:1, 18:20], in_=self.dram["L2_topk_v"])]
            fidx.n_dma = 2
            kb.dma_task("ge", fidx, writes=("f2idx",), tag="f2_idx")
            def fcv(en):
                return en.tensor_copy(out=mb[0:1, 14:16], in_=mb[0:1, 16:18])
            kb.task("ve", fcv, reads=("f2idx",), writes=("f2idxf",), tag="f2_cv")
            for k in range(2):
                for e in range(8):
                    def fme(en, k=k, e=e):
                        return en.tensor_scalar(out=mb[0:1, k * 8 + e: k * 8 + e + 1],
                                                in0=mb[0:1, 14 + k: 15 + k],
                                                scalar1=float(e), scalar2=None,
                                                op0=ALU.is_equal)
                    kb.task("ve", fme, reads=("f2idxf",), writes=("f2m",), tag=f"f2_m{k}{e}")
            def fmb(en):
                return en.matmul(pb[4][:, 0:16], self.sb["ones_f32"][0:1, 0:128],
                                 mb[0:1, 0:16], start=True, stop=False)
            kb.task("te", fmb, reads=("f2m",), writes=("pb4",), tag="f2_mb")
            def fvb(en):
                return en.matmul(pb[4][:, 16:18], self.sb["ones_f32"][0:1, 0:128],
                                 mb[0:1, 18:20], start=False, stop=True)
            kb.task("te", fvb, reads=("f2idx",), writes=("pb4",), tag="f2_vb")
            def fmbc(en):
                return en.tensor_copy(out=mb[:, 0:18], in_=pb[4][:, 0:18])
            kb.task("ve", fmbc, reads=("pb4",), writes=("f2mb",), tag="f2_mbc")

            As = self.dram["L2_AsT"]
            Bs = self.dram["L2_BsT"]
            for k in range(2):
                def fz(en, k=k):
                    return en.memset(Asel[:, k * 128:(k + 1) * 128], 0.0)
                kb.task("ve", fz, writes=(f"f2A{k}",), tag=f"f2_Az{k}")
                def fzB(en, k=k):
                    return en.memset(Bsel[0:16, k * 4096:(k + 1) * 4096], 0.0)
                kb.task("ve", fzB, writes=(f"f2B{k}",), tag=f"f2_Bz{k}")
            for e in range(8):
                for bh in range(2):
                    def fla(en, e=e, bh=bh):
                        return [en.dma_start(out=Ble[0:16, 0:2048],
                                             in_=Bs[e][:, bh * 2048:(bh + 1) * 2048])]
                    kb.dma_task("ge", fla, writes=("f2Ble",), tag=f"f2_Bl{e}_{bh}")
                    for k in range(2):
                        def fsB(en, k=k, e=e, bh=bh):
                            return en.scalar_tensor_tensor(
                                out=Bsel[0:16, k * 4096 + bh * 2048: k * 4096 + bh * 2048 + 2048],
                                in0=Ble[0:16, 0:2048], scalar=mb[0:16, k * 8 + e: k * 8 + e + 1],
                                in1=Bsel[0:16, k * 4096 + bh * 2048: k * 4096 + bh * 2048 + 2048],
                                op0=ALU.mult, op1=ALU.add)
                        kb.task("ve", fsB, reads=("f2Ble", "f2mb", f"f2B{k}"),
                                writes=(f"f2B{k}",), tag=f"f2_Bs{e}{k}{bh}")
                def flA(en, e=e):
                    return [en.dma_start(out=wbuf[:, 3072:3200].rearrange("p (ci r) -> p ci r", ci=8),
                                         in_=As[e].rearrange("(ci p) r -> p ci r", p=128))]
                kb.dma_task("ge", flA, writes=("f2wA",), tag=f"f2_Al{e}")
                for k in range(2):
                    def fsA(en, k=k, e=e):
                        return en.scalar_tensor_tensor(
                            out=Asel[:, k * 128:(k + 1) * 128],
                            in0=wbuf[:, 3072:3200], scalar=mb[:, k * 8 + e: k * 8 + e + 1],
                            in1=Asel[:, k * 128:(k + 1) * 128],
                            op0=ALU.mult, op1=ALU.add)
                    kb.task("ve", fsA, reads=("f2wA", "f2mb", f"f2A{k}"),
                            writes=(f"f2A{k}",), tag=f"f2_As{e}{k}")

            for k in range(2):
                for tc in range(4):
                    ps, psk = pb[tc % 2], f"pb{tc % 2}"
                    def fm(en, k=k, tc=tc, ps=ps):
                        last = None
                        for ci in range(8):
                            last = en.matmul(ps[0:16, 0:512],
                                             Asel[:, k * 128 + ci * 16: k * 128 + ci * 16 + 16],
                                             xnfm[:, ci * T + tc * 512: ci * T + tc * 512 + 512],
                                             start=(ci == 0), stop=(ci == 7))
                        return last
                    rd = (f"f2A{k}",) + tuple(kk for ci in range(8) for kk in self.xnk(ci, tc * 512, 512))
                    kb.task("te", fm, reads=rd, writes=(psk,), tag=f"f2_l1m{k}_{tc}")
                    def fe(en, k=k, tc=tc, ps=ps):
                        return en.tensor_copy(out=l1f[0:16, k * 2048 + tc * 512: k * 2048 + tc * 512 + 512],
                                              in_=ps[0:16, 0:512])
                    kb.task("ve", fe, reads=(psk,), writes=(f"f2l1{k}",), tag=f"f2_l1e{k}_{tc}")

            wiT = self.dram["L2_wiT"].rearrange("(ci p) f -> p ci f", p=128)
            woT = self.dram["L2_woT"].rearrange("(fi p) d -> p fi d", p=128)
            for tc in range(4):
                for fc in range(32):
                    sl = fc % 2
                    wap = wbuf[:, sl * 1024: sl * 1024 + 1024]
                    wkey = f"f2w{sl}"
                    def fw(en, fc=fc, wap=wap):
                        return [en.dma_start(out=wap.rearrange("p (ci o) -> p ci o", ci=8),
                                             in_=wiT[:, :, fc * 128:(fc + 1) * 128])]
                    kb.dma_task("ge", fw, writes=(wkey,), tag=f"f2_wi{tc}_{fc}")
                    for k in range(2):
                        ps, psk = pb[2 + k], f"pb{2 + k}"
                        def fm(en, k=k, tc=tc, fc=fc, ps=ps, wap=wap):
                            en.matmul(ps[:, 0:512],
                                      Bsel[0:16, k * 4096 + fc * 128: k * 4096 + fc * 128 + 128],
                                      l1f[0:16, k * 2048 + tc * 512: k * 2048 + tc * 512 + 512],
                                      start=True, stop=False)
                            last = None
                            for ci in range(8):
                                last = en.matmul(ps[:, 0:512],
                                                 wap[:, ci * 128:(ci + 1) * 128],
                                                 xnfm[:, ci * T + tc * 512: ci * T + tc * 512 + 512],
                                                 start=False, stop=(ci == 7))
                            return last
                        rd = (wkey, f"f2B{k}", f"f2l1{k}") + tuple(
                            kk for ci in range(8) for kk in self.xnk(ci, tc * 512, 512))
                        kb.task("te", fm, reads=rd, writes=(psk,), tag=f"f2_bm{tc}_{fc}_{k}")
                    def fc1(en, fc=fc):
                        return en.tensor_scalar(out=scomb[:, fc * 512:(fc + 1) * 512],
                                                in0=pb[2][:, 0:512], scalar1=0.0,
                                                scalar2=mb[:, 16:17],
                                                op0=ALU.max, op1=ALU.mult)
                    kb.task("ve", fc1, reads=("pb2", "f2mb"), writes=(f"f2sc{fc}",),
                            tag=f"f2_c1{tc}_{fc}")
                    def fc2(en, fc=fc):
                        return en.tensor_scalar(out=tscr[:, 0:512],
                                                in0=pb[3][:, 0:512], scalar1=0.0,
                                                scalar2=mb[:, 17:18],
                                                op0=ALU.max, op1=ALU.mult)
                    kb.task("ve", fc2, reads=("pb3", "f2mb"), writes=("f2tscr",),
                            tag=f"f2_c2{tc}_{fc}")
                    def fc3(en, fc=fc):
                        return en.tensor_tensor(out=scomb[:, fc * 512:(fc + 1) * 512],
                                                in0=scomb[:, fc * 512:(fc + 1) * 512],
                                                in1=tscr[:, 0:512], op=ALU.add)
                    kb.task("ve", fc3, reads=("f2tscr", f"f2sc{fc}"), writes=(f"f2sc{fc}",),
                            tag=f"f2_c3{tc}_{fc}")
                for hf in range(2):
                    for fi in range(32):
                        sl = fi % 2
                        wap2 = wbuf[:, 2048 + sl * 512: 2048 + sl * 512 + 512]
                        wkey2 = f"f2d{sl}"
                        def fw2(en, fi=fi, hf=hf, wap2=wap2):
                            return [en.dma_start(out=wap2,
                                                 in_=woT[:, fi, hf * 512:(hf + 1) * 512])]
                        kb.dma_task("ge", fw2, writes=(wkey2,), tag=f"f2_wo{tc}_{hf}_{fi}")
                        def fm(en, fi=fi, ps=None, wap2=wap2):
                            last = None
                            for tt in range(4):
                                last = en.matmul(
                                    pb[tt][:, 0:512],
                                    scomb[:, fi * 512 + tt * 128: fi * 512 + tt * 128 + 128],
                                    wap2, start=(fi == 0), stop=(fi == 31))
                            return last
                        kb.task("te", fm, reads=(wkey2, f"f2sc{fi}"),
                                writes=("pb0", "pb1", "pb2", "pb3"),
                                tag=f"f2_dm{tc}_{hf}_{fi}")
                    for tt in range(4):
                        tg = tc * 4 + tt
                        def fsg(en, tt=tt):
                            return en.activation(out=sil[:, 0:512], in_=pb[tt][:, 0:512],
                                                 func=AF.Silu)
                        kb.task("se", fsg, reads=(f"pb{tt}",), writes=("lnscr",),
                                tag=f"f2_xg{tc}_{hf}_{tt}")
                        def fmul(en, tg=tg, hf=hf):
                            return en.tensor_tensor(out=sil[:, 0:512], in0=sil[:, 0:512],
                                                    in1=br[:, tg * D + hf * 512: tg * D + hf * 512 + 512],
                                                    op=ALU.mult)
                        kb.task("ve", fmul, reads=("lnscr", f"br{tg}"), writes=("lnscr",),
                                tag=f"f2_xm{tc}_{hf}_{tt}")
                        def fadd(en, tg=tg, hf=hf):
                            return en.tensor_tensor(
                                out=x[:, tg * D + hf * 512: tg * D + hf * 512 + 512],
                                in0=x[:, tg * D + hf * 512: tg * D + hf * 512 + 512],
                                in1=sil[:, 0:512], op=ALU.add)
                        kb.task("ve", fadd, reads=("lnscr", self.xkey(tg)),
                                writes=(self.xkey(tg),), tag=f"f2_xa{tc}_{hf}_{tt}")


    def emit_quant(self, tag, src_ap, src_keys, dst_ap, dst_keys, n, qs, qkeys,
                   parts=128):
        kb = self.kb
        nb = n // 16
        a, r, scl = qs["a"], qs["r"], qs["scl"]
        ak, rk, sk = qkeys
        M = 12582912.0
        def q1(en):
            src3 = bass.AP(src_ap.tensor, src_ap.offset,
                           [src_ap.ap[0], [16, nb], [1, 16]])
            return en.reduce_max(scl[0:parts, 0:nb], src3, axis=AX.X,
                                 apply_absolute_value=True)
        kb.task("ve", q1, reads=src_keys, writes=(sk,), tag=f"{tag}q1")
        def q2(en):
            return en.tensor_scalar(out=scl[0:parts, 0:nb], in0=scl[0:parts, 0:nb],
                                    scalar1=1.0 / 6.0, scalar2=1e-12,
                                    op0=ALU.mult, op1=ALU.max)
        kb.task("ve", q2, reads=(sk,), writes=(sk,), tag=f"{tag}q2")
        def q3(en):
            return en.reciprocal(scl[0:parts, nb:2 * nb], scl[0:parts, 0:nb])
        kb.task("ve", q3, reads=(sk,), writes=(sk,), tag=f"{tag}q3")
        def q4(en):
            return en.activation(out=a[0:parts, 0:n], in_=src_ap, func=AF.Abs)
        kb.task("se", q4, reads=src_keys, writes=(ak,), tag=f"{tag}q4")
        def rep(col0):
            return bass.AP(scl[:].tensor, scl[:].offset + col0,
                           [scl[:].ap[0], [1, nb], [0, 16]])
        def q5(en):
            return en.tensor_tensor(out=a[0:parts, 0:n], in0=a[0:parts, 0:n],
                                    in1=rep(nb), op=ALU.mult)
        kb.task("ve", q5, reads=(ak, sk), writes=(ak,), tag=f"{tag}q5")
        def q6(en):
            return en.tensor_scalar(out=r[0:parts, 0:n], in0=a[0:parts, 0:n],
                                    scalar1=2.0, scalar2=2.0, op0=ALU.min, op1=ALU.mult)
        kb.task("ve", q6, reads=(ak,), writes=(rk,), tag=f"{tag}q6")
        def q7(en):
            return en.tensor_scalar(out=r[0:parts, 0:n], in0=r[0:parts, 0:n],
                                    scalar1=M, scalar2=M, op0=ALU.add, op1=ALU.subtract)
        kb.task("ve", q7, reads=(rk,), writes=(rk,), tag=f"{tag}q7")
        def q8(en):
            return en.tensor_scalar(out=a[0:parts, 0:n], in0=a[0:parts, 0:n],
                                    scalar1=2.5, scalar2=None, op0=ALU.is_ge,
                                    accum_out=None) if False else en.scalar_tensor_tensor(
                out=r[0:parts, 0:n], in0=a[0:parts, 0:n], scalar=0.0,
                in1=r[0:parts, 0:n], op0=ALU.bypass, op1=ALU.add) if False else None
        def q8a(en):
            return en.tensor_scalar(out=qs["p"][0:parts, 0:n], in0=a[0:parts, 0:n],
                                    scalar1=2.5, scalar2=None, op0=ALU.is_ge)
        kb.task("ve", q8a, reads=(ak,), writes=(qkeys[0] + "p",), tag=f"{tag}q8a")
        def q8b(en):
            return en.scalar_tensor_tensor(out=qs["p"][0:parts, 0:n], in0=a[0:parts, 0:n],
                                           scalar=3.5, in1=qs["p"][0:parts, 0:n],
                                           op0=ALU.is_ge, op1=ALU.add)
        kb.task("ve", q8b, reads=(ak, qkeys[0] + "p"), writes=(qkeys[0] + "p",), tag=f"{tag}q8b")
        def q8c(en):
            return en.scalar_tensor_tensor(out=qs["p"][0:parts, 0:n], in0=a[0:parts, 0:n],
                                           scalar=5.0, in1=qs["p"][0:parts, 0:n],
                                           op0=ALU.is_ge, op1=ALU.add)
        kb.task("ve", q8c, reads=(ak, qkeys[0] + "p"), writes=(qkeys[0] + "p",), tag=f"{tag}q8c")
        def q8d(en):
            return en.scalar_tensor_tensor(out=qs["p"][0:parts, 0:n], in0=a[0:parts, 0:n],
                                           scalar=5.0, in1=qs["p"][0:parts, 0:n],
                                           op0=ALU.is_ge, op1=ALU.add)
        kb.task("ve", q8d, reads=(ak, qkeys[0] + "p"), writes=(qkeys[0] + "p",), tag=f"{tag}q8d")
        def q9(en):
            return en.scalar_tensor_tensor(out=r[0:parts, 0:n], in0=qs["p"][0:parts, 0:n],
                                           scalar=2.0, in1=r[0:parts, 0:n],
                                           op0=ALU.mult, op1=ALU.add)
        kb.task("ve", q9, reads=(qkeys[0] + "p", rk), writes=(rk,), tag=f"{tag}q9")
        def q10(en):
            return en.activation(out=a[0:parts, 0:n], in_=src_ap, func=AF.Sign)
        kb.task("se", q10, reads=src_keys, writes=(ak,), tag=f"{tag}q10")
        def q11(en):
            return en.tensor_scalar(out=scl[0:parts, 2 * nb:3 * nb],
                                    in0=scl[0:parts, 0:nb],
                                    scalar1=0.5, scalar2=None, op0=ALU.mult)
        kb.task("ve", q11, reads=(sk,), writes=(sk,), tag=f"{tag}q11")
        def q12(en):
            return en.tensor_tensor(out=r[0:parts, 0:n], in0=r[0:parts, 0:n],
                                    in1=rep(2 * nb), op=ALU.mult)
        kb.task("ve", q12, reads=(rk, sk), writes=(rk,), tag=f"{tag}q12")
        def q13(en):
            return en.tensor_tensor(out=dst_ap, in0=r[0:parts, 0:n],
                                    in1=a[0:parts, 0:n], op=ALU.mult)
        kb.task("ve", q13, reads=(rk, ak), writes=dst_keys, tag=f"{tag}q13")

    def emit_l3_park(self):
        import contextlib
        nc, kb = self.nc, self.kb
        pb = [self.psum[f"pb{i}"] for i in range(6)]
        ident = self.sb["ident_bf"]
        self.kb.fence()
        with contextlib.ExitStack() as es:
            wsrc = es.enter_context(nc.sbuf_tensor("pkw", [128, 2816], F32))
            qa = es.enter_context(nc.sbuf_tensor("pka", [128, 2816], F32))
            qr = es.enter_context(nc.sbuf_tensor("pkr", [128, 2816], F32))
            qp = es.enter_context(nc.sbuf_tensor("pkp", [128, 2816], F32))
            qscl = es.enter_context(nc.sbuf_tensor("pks", [128, 3 * 176], F32))
            wq = es.enter_context(nc.sbuf_tensor("pkq", [128, 2816], BF16))
            wt = es.enter_context(nc.sbuf_tensor("pkt", [128, 2816], BF16))
            qs = {"a": qa, "r": qr, "scl": qscl, "p": qp}
            for wname, dst, nt, ncol in (("L3_wg_nat", self.wgq_d, 22, 1024),
                                         ("L3_wu_nat", self.wuq_d, 22, 1024),
                                         ("L3_wd_nat", self.wdq_d, 8, 2816)):
                wsrc_d = self.dram[wname]
                nct = ncol // 128
                for j in range(nt):
                    def fl(en, j=j, wsrc_d=wsrc_d, ncol=ncol):
                        return [en.dma_start(out=wsrc[:, 0:ncol],
                                             in_=wsrc_d[j * 128:(j + 1) * 128, :])]
                    kb.dma_task("sq", fl, writes=("pkw",), tag=f"pk_l{wname}{j}")
                    self.emit_quant(f"pk{wname}{j}", wsrc[:, 0:ncol], ("pkw",),
                                    wq[:, 0:ncol], ("pkq",), ncol, qs,
                                    ("pka", "pkr", "pks"))
                    for half in range((nct + 7) // 8):
                        clo, chi = half * 8, min(nct, half * 8 + 8)
                        pt = self.psum[f"pt{half % 2}"]
                        ptk = f"pt{half % 2}"
                        def ftr(en, j=j, clo=clo, chi=chi, pt=pt):
                            last = None
                            for c in range(clo, chi):
                                last = en.transpose(pt[:, (c - clo) * 128:(c - clo) * 128 + 128],
                                                    wq[:, c * 128:(c + 1) * 128], ident[:])
                            return last
                        kb.task("te", ftr, reads=("pkq",), writes=(ptk,), tag=f"pk_t{wname}{j}_{half}")
                        def fev(en, clo=clo, chi=chi, pt=pt):
                            return en.tensor_copy(out=wt[:, clo * 128: clo * 128 + (chi - clo) * 128],
                                                  in_=pt[:, 0:(chi - clo) * 128])
                        kb.task("ve", fev, reads=(ptk,), writes=("pkt",), tag=f"pk_e{wname}{j}_{half}")
                    def fst(en, j=j, dst=dst, ncol=ncol):
                        return [en.dma_start(
                            out=dst.rearrange("(ci p) f -> p ci f", p=128)[:, :, j * 128:(j + 1) * 128],
                            in_=wt[:, 0:ncol].rearrange("p (ci q) -> p ci q", q=128))]
                    kb.dma_task("sq", fst, reads=("pkt",), writes=(f"pkd{wname}", f"pkd{wname}_{j}"),
                                tag=f"pk_s{wname}{j}")
            def fdr(en):
                return en.nop(nofuse=True)
            allst = tuple(f"pkdL3_wg_nat_{j}" for j in range(22)) + \
                    tuple(f"pkdL3_wu_nat_{j}" for j in range(22)) + \
                    tuple(f"pkdL3_wd_nat_{j}" for j in range(8))
            kb.task("ge", fdr, reads=allst, tag="pk_drain")

    def emit_l3_attn(self, seq):
        import contextlib
        nc, kb = self.nc, self.kb
        xnfm = self.sb["xnfm"]
        br = self.sb["br"]
        onesc = self.sb["ones_col"]
        onesf = self.sb["ones_f32"]
        pb = [self.psum[f"pb{i}"] for i in range(6)]
        CH = 256
        NCH = T // CH
        DI = 2048
        NJ = 16

        self.kb.fence()
        with contextlib.ExitStack() as es:
            xpq = es.enter_context(nc.sbuf_tensor(f"l3xp_{seq}", [128, NJ * (CH + 4)], BF16))
            zq = es.enter_context(nc.sbuf_tensor(f"l3z_{seq}", [128, NJ * CH], BF16))
            bcb = es.enter_context(nc.sbuf_tensor(f"l3bc_{seq}", [128, 32 * CH], BF16))
            ssmB = es.enter_context(nc.sbuf_tensor(f"l3sB_{seq}", [16, 256], BF16))
            ssmC = es.enter_context(nc.sbuf_tensor(f"l3sC_{seq}", [16, 256], BF16))
            l3row = es.enter_context(nc.sbuf_tensor(f"l3rw_{seq}", [1, 1024], F32))
            ind16 = es.enter_context(nc.sbuf_tensor(f"l3in_{seq}", [16, 2048], BF16))
            brq = es.enter_context(nc.sbuf_tensor(f"l3br_{seq}", [128, CH], F32))
            delt = es.enter_context(nc.sbuf_tensor(f"l3dl_{seq}", [128, CH], F32))
            duq = es.enter_context(nc.sbuf_tensor(f"l3du_{seq}", [128, CH], F32))
            dAq = es.enter_context(nc.sbuf_tensor(f"l3dA_{seq}", [128, CH], F32))
            dbu = es.enter_context(nc.sbuf_tensor(f"l3db_{seq}", [128, CH], F32))
            stq = es.enter_context(nc.sbuf_tensor(f"l3st_{seq}", [128, CH], F32))
            tmp = es.enter_context(nc.sbuf_tensor(f"l3tm_{seq}", [128, CH], F32))
            carry = es.enter_context(nc.sbuf_tensor(f"l3cr_{seq}", [128, 256], F32))
            tail = es.enter_context(nc.sbuf_tensor(f"l3tl_{seq}", [128, NJ * 4], BF16))
            mrb = es.enter_context(nc.sbuf_tensor(f"l3mr_{seq}", [128, 2 * CH], F32))
            Acol = es.enter_context(nc.sbuf_tensor(f"l3A_{seq}", [128, 256], F32))
            cvw = es.enter_context(nc.sbuf_tensor(f"l3cv_{seq}", [128, 176], F32))
            xprj = es.enter_context(nc.sbuf_tensor(f"l3xw_{seq}", [128, NJ * 33], BF16))
            wbuf = es.enter_context(nc.sbuf_tensor(f"l3w_{seq}", [128, 2 * 1024], BF16))
            scr = es.enter_context(nc.sbuf_tensor(f"l3sc_{seq}", [128, 512], BF16))

            self.emit_ln(f"l3ln_{seq}")

            def fcv(en):
                return [en.dma_start(out=cvw[:, 0:64], in_=self.dram["L3_convw"]),
                        en.dma_start(out=cvw[:, 64:80], in_=self.dram["L3_convb"]),
                        en.dma_start(out=cvw[:, 80:96], in_=self.dram["L3_dtw"]),
                        en.dma_start(out=cvw[:, 96:112], in_=self.dram["L3_dtb"]),
                        en.dma_start(out=cvw[:, 112:128], in_=self.dram["L3_D"]),
                        en.dma_start(out=cvw[:, 128:144], in_=self.dram["L3_normw"]),
                        en.dma_start(out=cvw[:, 144:160], in_=self.dram["L3_normb"])]
            fcv.n_dma = 7
            kb.dma_task("ge", fcv, writes=("l3cv",), tag="l3_cv")
            def fA0(en):
                return [en.dma_start(out=Acol[:, 0:256], in_=self.dram["L3_logA"])]
            kb.dma_task("ge", fA0, writes=("l3A",), tag="l3_A0")
            def fA1(en):
                return en.activation(out=Acol[:, 0:256], in_=Acol[:, 0:256], func=AF.Exp)
            kb.task("se", fA1, reads=("l3A",), writes=("l3A",), tag="l3_A1")
            def fA2(en):
                return en.tensor_scalar(out=Acol[:, 0:256], in0=Acol[:, 0:256],
                                        scalar1=-1.0, scalar2=None, op0=ALU.mult)
            kb.task("ve", fA2, reads=("l3A",), writes=("l3A",), tag="l3_A2")
            def fxp(en):
                return [en.dma_start(
                    out=xprj[:].rearrange("p (j o) -> p j o", j=NJ),
                    in_=self.dram["L3_xprojT"].rearrange("(j p) o -> p j o", p=128))]
            kb.dma_task("ge", fxp, writes=("l3xprj",), tag="l3_xprj")
            def find16(en):
                return [en.dma_start(out=ind16[:], in_=self.dram["c_ind16"])]
            kb.dma_task("ge", find16, writes=("l3ind",), tag="l3_ind")
            def fcz(en):
                return en.memset(carry[:, 0:256], 0.0)
            kb.task("ve", fcz, writes=("l3carry",), tag="l3_cz")
            def ftz(en):
                return en.memset(tail[:, 0:NJ * 4], 0.0)
            kb.task("ve", ftz, writes=("l3tail",), tag="l3_tz")

            inpT = self.dram["L3_inpT"].rearrange("(ci p) o -> p ci o", p=128)
            outpT = self.dram["L3_outpT"].rearrange("(j p) o -> p j o", p=128)

            for q in range(NCH):
                for oc in range(32):
                    sl = oc % 2
                    wap = wbuf[:, sl * 1024: sl * 1024 + 1024]
                    wkey = f"l3w{sl}"
                    def fw(en, oc=oc, wap=wap):
                        return [en.dma_start(out=wap.rearrange("p (ci o) -> p ci o", ci=8),
                                             in_=inpT[:, :, oc * 128:(oc + 1) * 128])]
                    kb.dma_task("ge", fw, writes=(wkey,), tag=f"l3_iw{q}_{oc}")
                    ps, psk = pb[oc % 2], f"pb{oc % 2}"
                    def fm(en, q=q, ps=ps, wap=wap):
                        last = None
                        for ci in range(8):
                            last = en.matmul(ps[:, 0:CH],
                                             wap[:, ci * 128:(ci + 1) * 128],
                                             xnfm[:, ci * T + q * CH: ci * T + q * CH + CH],
                                             start=(ci == 0), stop=(ci == 7))
                        return last
                    rd = (wkey,) + tuple(k for ci in range(8) for k in self.xnk(ci, q * CH, CH))
                    kb.task("te", fm, reads=rd, writes=(psk,), tag=f"l3_im{q}_{oc}")
                    if oc < 16:
                        def fe(en, oc=oc, ps=ps):
                            return en.copy(
                                out=xpq[:, oc * (CH + 4) + 4: oc * (CH + 4) + 4 + CH],
                                in_=ps[:, 0:CH])
                        kb.task("se", fe, reads=(psk,), writes=(f"l3xc{oc}",),
                                tag=f"l3_ie{q}_{oc}")
                    else:
                        def fe(en, oc=oc, ps=ps):
                            return en.copy(
                                out=zq[:, (oc - 16) * CH: (oc - 16) * CH + CH],
                                in_=ps[:, 0:CH])
                        kb.task("se", fe, reads=(psk,), writes=(f"l3zq{oc - 16}",),
                                tag=f"l3_ie{q}_{oc}")
                for j in range(NJ):
                    def ft1(en, j=j):
                        return en.tensor_copy(out=xpq[:, j * (CH + 4) + 1: j * (CH + 4) + 4],
                                              in_=tail[:, j * 4: j * 4 + 3])
                    kb.task("ve", ft1, reads=("l3tail", f"l3xc{j}"), writes=(f"l3xc{j}",),
                            tag=f"l3_tc{q}_{j}")
                    def ft2(en, j=j):
                        return en.tensor_copy(out=tail[:, j * 4: j * 4 + 3],
                                              in_=xpq[:, j * (CH + 4) + 4 + CH - 3: j * (CH + 4) + 4 + CH])
                    kb.task("ve", ft2, reads=(f"l3xc{j}",), writes=("l3tail",),
                            tag=f"l3_ts{q}_{j}")
                for j in range(NJ):
                    base = j * (CH + 4)
                    def c1(en, j=j, base=base):
                        return en.tensor_scalar(out=delt[:, 0:CH],
                                                in0=xpq[:, base + 1: base + 1 + CH],
                                                scalar1=cvw[:, j * 4: j * 4 + 1],
                                                scalar2=None, op0=ALU.mult)
                    kb.task("ve", c1, reads=(f"l3xc{j}", "l3cv"), writes=("l3delt",),
                            tag=f"l3_c1{q}_{j}")
                    for kk in range(1, 4):
                        def ck(en, j=j, base=base, kk=kk):
                            return en.scalar_tensor_tensor(
                                out=delt[:, 0:CH],
                                in0=xpq[:, base + 1 + kk: base + 1 + kk + CH],
                                scalar=cvw[:, j * 4 + kk: j * 4 + kk + 1],
                                in1=delt[:, 0:CH], op0=ALU.mult, op1=ALU.add)
                        kb.task("ve", ck, reads=(f"l3xc{j}", "l3cv", "l3delt"),
                                writes=("l3delt",), tag=f"l3_c{kk+1}{q}_{j}")
                    def c5(en, j=j, base=base):
                        return en.activation(out=xpq[:, base + 4: base + 4 + CH],
                                             in_=delt[:, 0:CH], func=AF.Silu,
                                             bias=cvw[:, 64 + j: 64 + j + 1], scale=1.0)
                    kb.task("se", c5, reads=("l3delt", "l3cv"), writes=(f"l3xc{j}",),
                            tag=f"l3_c5{q}_{j}")
                def fsm(en, q=q):
                    last = None
                    for j in range(NJ):
                        en.matmul(pb[2][0:16, 0:CH], xprj[:, j * 33: j * 33 + 16],
                                  xpq[:, j * (CH + 4) + 4: j * (CH + 4) + 4 + CH],
                                  start=(j == 0), stop=(j == NJ - 1))
                        en.matmul(pb[3][0:16, 0:CH], xprj[:, j * 33 + 16: j * 33 + 32],
                                  xpq[:, j * (CH + 4) + 4: j * (CH + 4) + 4 + CH],
                                  start=(j == 0), stop=(j == NJ - 1))
                        last = en.matmul(pb[5][0:1, 0:CH], xprj[:, j * 33 + 32: j * 33 + 33],
                                         xpq[:, j * (CH + 4) + 4: j * (CH + 4) + 4 + CH],
                                         start=(j == 0), stop=(j == NJ - 1))
                    return last
                kb.task("te", fsm, reads=("l3xprj",) + tuple(f"l3xc{j}" for j in range(NJ)),
                        writes=("pb2", "pb3", "pb5"), tag=f"l3_sm{q}")
                def fseB(en):
                    return en.tensor_copy(out=ssmB[0:16, 0:CH], in_=pb[2][0:16, 0:CH])
                kb.task("ve", fseB, reads=("pb2",), writes=("l3ssmB",), tag=f"l3_seB{q}")
                def fseC(en):
                    return en.tensor_copy(out=ssmC[0:16, 0:CH], in_=pb[3][0:16, 0:CH])
                kb.task("ve", fseC, reads=("pb3",), writes=("l3ssmC",), tag=f"l3_seC{q}")
                def fseD(en):
                    return en.tensor_copy(out=l3row[0:1, 2 * CH:3 * CH], in_=pb[5][0:1, 0:CH])
                kb.task("ve", fseD, reads=("pb5",), writes=("l3drow",), tag=f"l3_seD{q}")
                def fbr(en):
                    return en.matmul(pb[3][:, 0:CH], onesf[0:1, 0:128],
                                     l3row[0:1, 2 * CH:3 * CH], start=True, stop=True)
                kb.task("te", fbr, reads=("l3drow",), writes=("pb3",), tag=f"l3_br{q}")
                def fbe(en):
                    return en.tensor_copy(out=brq[:, 0:CH], in_=pb[3][:, 0:CH])
                kb.task("ve", fbe, reads=("pb3",), writes=("l3brq",), tag=f"l3_be{q}")
                for n in range(16):
                    def fb1(en, n=n):
                        en.matmul(pb[3][:, 0:CH], ind16[0:16, n * 128:(n + 1) * 128],
                                  ssmB[0:16, 0:CH], start=True, stop=True)
                        return en.matmul(pb[4][:, 0:CH], ind16[0:16, n * 128:(n + 1) * 128],
                                         ssmC[0:16, 0:CH], start=True, stop=True)
                    kb.task("te", fb1, reads=("l3ssmB", "l3ssmC", "l3ind"), writes=("pb3", "pb4"),
                            tag=f"l3_bb{q}_{n}")
                    def fb2(en, n=n):
                        return en.tensor_copy(out=bcb[:, n * CH:(n + 1) * CH],
                                              in_=pb[3][:, 0:CH])
                    kb.task("ve", fb2, reads=("pb3",), writes=("l3bcb",), tag=f"l3_b2{q}_{n}")
                    def fb3(en, n=n):
                        return en.tensor_copy(out=bcb[:, (16 + n) * CH:(17 + n) * CH],
                                              in_=pb[4][:, 0:CH])
                    kb.task("ve", fb3, reads=("pb4",), writes=("l3bcb",), tag=f"l3_b3{q}_{n}")
                for j in range(NJ):
                    def d0(en, j=j):
                        return en.tensor_scalar(out=delt[:, 0:CH], in0=brq[:, 0:CH],
                                                scalar1=cvw[:, 80 + j: 81 + j],
                                                scalar2=cvw[:, 96 + j: 97 + j],
                                                op0=ALU.mult, op1=ALU.add)
                    kb.task("ve", d0, reads=("l3brq", "l3cv"), writes=("l3delt",),
                            tag=f"l3_d0{q}_{j}")
                    def d0b(en):
                        return en.tensor_scalar(out=delt[:, 0:CH], in0=delt[:, 0:CH],
                                                scalar1=80.0, scalar2=None, op0=ALU.min)
                    kb.task("ve", d0b, reads=("l3delt",), writes=("l3delt",),
                            tag=f"l3_d0b{q}_{j}")
                    def d1(en):
                        return en.activation(out=delt[:, 0:CH], in_=delt[:, 0:CH],
                                             func=AF.Exp)
                    kb.task("se", d1, reads=("l3delt",), writes=("l3delt",),
                            tag=f"l3_d1{q}_{j}")
                    def d2(en):
                        return en.activation(out=delt[:, 0:CH], in_=delt[:, 0:CH],
                                             func=AF.Ln, bias=self.cs(1.0), scale=1.0)
                    kb.task("se", d2, reads=("l3delt",), writes=("l3delt",),
                            tag=f"l3_d2{q}_{j}")
                    def d3(en):
                        return en.tensor_scalar(out=delt[:, 0:CH], in0=delt[:, 0:CH],
                                                scalar1=0.001, scalar2=0.1,
                                                op0=ALU.max, op1=ALU.min)
                    kb.task("ve", d3, reads=("l3delt",), writes=("l3delt",),
                            tag=f"l3_d3{q}_{j}")
                    def d4(en, j=j):
                        return en.tensor_tensor(out=duq[:, 0:CH], in0=delt[:, 0:CH],
                                                in1=xpq[:, j * (CH + 4) + 4: j * (CH + 4) + 4 + CH],
                                                op=ALU.mult)
                    kb.task("ve", d4, reads=("l3delt", f"l3xc{j}"), writes=("l3duq",),
                            tag=f"l3_d4{q}_{j}")
                    yb, ybk = pb[4 + j % 2], f"pb{4 + j % 2}"
                    def d5(en, j=j, yb=yb):
                        return en.tensor_scalar(out=yb[:, 0:CH],
                                                in0=xpq[:, j * (CH + 4) + 4: j * (CH + 4) + 4 + CH],
                                                scalar1=cvw[:, 112 + j: 113 + j],
                                                scalar2=None, op0=ALU.mult)
                    kb.task("ve", d5, reads=(f"l3xc{j}", "l3cv"), writes=(ybk,),
                            tag=f"l3_d5{q}_{j}")
                    for n in range(16):
                        def s1(en, j=j, n=n):
                            return en.activation(out=dAq[:, 0:CH], in_=delt[:, 0:CH],
                                                 func=AF.Exp,
                                                 scale=Acol[:, j * 16 + n: j * 16 + n + 1])
                        kb.task("se", s1, reads=("l3delt", "l3A"), writes=("l3dA",),
                                tag=f"l3_s1{q}_{j}_{n}")
                        def s2(en, n=n):
                            return en.tensor_tensor(out=dbu[:, 0:CH], in0=duq[:, 0:CH],
                                                    in1=bcb[:, n * CH:(n + 1) * CH],
                                                    op=ALU.mult)
                        kb.task("ge", s2, reads=("l3duq", "l3bcb"), writes=("l3dbu",),
                                tag=f"l3_s2{q}_{j}_{n}")
                        def s3(en, j=j, n=n):
                            return en.tensor_tensor_scan(
                                out=stq[:, 0:CH], data0=dAq[:, 0:CH], data1=dbu[:, 0:CH],
                                initial=carry[:, j * 16 + n: j * 16 + n + 1],
                                op0=ALU.mult, op1=ALU.add)
                        kb.task("ve", s3, reads=("l3dA", "l3dbu", "l3carry"),
                                writes=("l3stq",), tag=f"l3_s3{q}_{j}_{n}")
                        def s4(en, j=j, n=n):
                            return en.tensor_copy(out=carry[:, j * 16 + n: j * 16 + n + 1],
                                                  in_=stq[:, CH - 1: CH])
                        kb.task("ve", s4, reads=("l3stq",), writes=("l3carry",),
                                tag=f"l3_s4{q}_{j}_{n}")
                        def s5(en, n=n):
                            return en.tensor_tensor(out=tmp[:, 0:CH], in0=stq[:, 0:CH],
                                                    in1=bcb[:, (16 + n) * CH:(17 + n) * CH],
                                                    op=ALU.mult)
                        kb.task("ge", s5, reads=("l3stq", "l3bcb"), writes=("l3tmp",),
                                tag=f"l3_s5{q}_{j}_{n}")
                        def s6(en, yb=yb):
                            return en.tensor_tensor(out=yb[:, 0:CH], in0=yb[:, 0:CH],
                                                    in1=tmp[:, 0:CH], op=ALU.add)
                        kb.task("ve", s6, reads=("l3tmp", ybk), writes=(ybk,),
                                tag=f"l3_s6{q}_{j}_{n}")
                    def d6(en, j=j, yb=yb):
                        return en.tensor_copy(out=xpq[:, j * (CH + 4) + 4: j * (CH + 4) + 4 + CH],
                                              in_=yb[:, 0:CH])
                    kb.task("ve", d6, reads=(ybk,), writes=(f"l3xc{j}",),
                            tag=f"l3_d6{q}_{j}")
                def e1(en):
                    last = None
                    for j in range(NJ):
                        last = en.matmul(pb[0][0:1, 0:CH], onesc[:, 0:1],
                                         xpq[:, j * (CH + 4) + 4: j * (CH + 4) + 4 + CH],
                                         start=(j == 0), stop=(j == NJ - 1))
                    return last
                kb.task("te", e1, reads=tuple(f"l3xc{j}" for j in range(NJ)),
                        writes=("pb0",), tag=f"l3_e1{q}")
                for j in range(NJ):
                    def e2(en, j=j):
                        return en.activation(out=scr[:, 0:CH],
                                             in_=xpq[:, j * (CH + 4) + 4: j * (CH + 4) + 4 + CH],
                                             func=AF.Square)
                    kb.task("se", e2, reads=(f"l3xc{j}",), writes=("l3scr",),
                            tag=f"l3_e2{q}_{j}")
                    def e3(en, j=j):
                        return en.matmul(pb[1][0:1, 0:CH], onesc[:, 0:1], scr[:, 0:CH],
                                         start=(j == 0), stop=(j == NJ - 1))
                    kb.task("te", e3, reads=("l3scr",), writes=("pb1",), tag=f"l3_e3{q}_{j}")
                def e4(en):
                    return en.tensor_scalar(out=l3row[0:1, 0:CH], in0=pb[0][0:1, 0:CH],
                                            scalar1=1.0 / 2048.0, scalar2=None, op0=ALU.mult)
                kb.task("ve", e4, reads=("pb0",), writes=("l3mean",), tag=f"l3_e4{q}")
                def e5(en):
                    return en.tensor_scalar(out=l3row[0:1, CH:2 * CH], in0=pb[1][0:1, 0:CH],
                                            scalar1=1.0 / 2048.0, scalar2=None, op0=ALU.mult)
                kb.task("ve", e5, reads=("pb1",), writes=("l3ey2",), tag=f"l3_e5{q}")
                def e6(en):
                    return en.activation(out=l3row[0:1, 2 * CH:3 * CH], in_=l3row[0:1, 0:CH],
                                         func=AF.Square)
                kb.task("se", e6, reads=("l3mean",), writes=("l3m2",), tag=f"l3_e6{q}")
                def e7(en):
                    return en.tensor_tensor(out=l3row[0:1, CH:2 * CH], in0=l3row[0:1, CH:2 * CH],
                                            in1=l3row[0:1, 2 * CH:3 * CH], op=ALU.subtract)
                kb.task("ve", e7, reads=("l3ey2", "l3m2"), writes=("l3var",), tag=f"l3_e7{q}")
                def e8(en):
                    return en.activation(out=l3row[0:1, CH:2 * CH], in_=l3row[0:1, CH:2 * CH],
                                         func=AF.Sqrt, bias=self.cs(1e-5)[0:1, 0:1], scale=1.0)
                kb.task("se", e8, reads=("l3var",), writes=("l3var",), tag=f"l3_e8{q}")
                def e9(en):
                    return en.reciprocal(l3row[0:1, CH:2 * CH], l3row[0:1, CH:2 * CH])
                kb.task("ve", e9, reads=("l3var",), writes=("l3var",), tag=f"l3_e9{q}")
                def e10(en):
                    en.matmul(pb[0][:, 0:CH], onesf[0:1, 0:128], l3row[0:1, 0:CH],
                              start=True, stop=True)
                    return en.matmul(pb[1][:, 0:CH], onesf[0:1, 0:128], l3row[0:1, CH:2 * CH],
                                     start=True, stop=True)
                kb.task("te", e10, reads=("l3mean", "l3var"), writes=("pb0", "pb1"),
                        tag=f"l3_e10{q}")
                def e11(en):
                    return en.tensor_copy(out=mrb[:, 0:CH], in_=pb[0][:, 0:CH])
                kb.task("ve", e11, reads=("pb0",), writes=("l3mrb",), tag=f"l3_e11{q}")
                def e12(en):
                    return en.tensor_copy(out=mrb[:, CH:2 * CH], in_=pb[1][:, 0:CH])
                kb.task("ve", e12, reads=("pb1",), writes=("l3mrb",), tag=f"l3_e12{q}")
                for j in range(NJ):
                    def e13(en, j=j):
                        return en.tensor_tensor(out=delt[:, 0:CH],
                                                in0=xpq[:, j * (CH + 4) + 4: j * (CH + 4) + 4 + CH],
                                                in1=mrb[:, 0:CH], op=ALU.subtract)
                    kb.task("ve", e13, reads=(f"l3xc{j}", "l3mrb"), writes=("l3delt",),
                            tag=f"l3_e13{q}_{j}")
                    def e14(en):
                        return en.tensor_tensor(out=delt[:, 0:CH], in0=delt[:, 0:CH],
                                                in1=mrb[:, CH:2 * CH], op=ALU.mult)
                    kb.task("ve", e14, reads=("l3delt", "l3mrb"), writes=("l3delt",),
                            tag=f"l3_e14{q}_{j}")
                    def e15(en, j=j):
                        return en.tensor_scalar(out=delt[:, 0:CH], in0=delt[:, 0:CH],
                                                scalar1=cvw[:, 128 + j: 129 + j],
                                                scalar2=cvw[:, 144 + j: 145 + j],
                                                op0=ALU.mult, op1=ALU.add)
                    kb.task("ve", e15, reads=("l3delt", "l3cv"), writes=("l3delt",),
                            tag=f"l3_e15{q}_{j}")
                    def e16(en, j=j):
                        return en.activation(out=scr[:, 0:CH],
                                             in_=zq[:, j * CH:(j + 1) * CH], func=AF.Silu)
                    kb.task("se", e16, reads=(f"l3zq{j}",), writes=("l3scr",),
                            tag=f"l3_e16{q}_{j}")
                    def e17(en, j=j):
                        return en.tensor_tensor(out=zq[:, j * CH:(j + 1) * CH],
                                                in0=delt[:, 0:CH], in1=scr[:, 0:CH],
                                                op=ALU.mult)
                    kb.task("ve", e17, reads=("l3delt", "l3scr", f"l3zq{j}"),
                            writes=(f"l3zq{j}",), tag=f"l3_e17{q}_{j}")
                for tt in range(CH // 128):
                    tg = q * (CH // 128) + tt
                    for hf in range(2):
                        ps, psk = pb[2 + hf], f"pb{2 + hf}"
                        def fm(en, tt=tt, hf=hf, ps=ps):
                            last = None
                            for j in range(NJ):
                                pass
                            return last
                        def fm2(en, tt=tt, hf=hf, ps=ps):
                            last = None
                            for j in range(NJ):
                                last = en.matmul(
                                    ps[:, 0:512],
                                    zq[:, j * CH + tt * 128: j * CH + tt * 128 + 128],
                                    wbuf[:, (j % 2) * 1024 + hf * 512: (j % 2) * 1024 + hf * 512 + 512],
                                    start=(j == 0), stop=(j == NJ - 1))
                            return last
                for j in range(NJ):
                    sl = j % 2
                    wap = wbuf[:, sl * 1024: sl * 1024 + 1024]
                    wkey = f"l3w{sl}"
                    def fw(en, j=j, wap=wap):
                        return [en.dma_start(out=wap, in_=outpT[:, j, :])]
                    kb.dma_task("ge", fw, writes=(wkey,), tag=f"l3_ow{q}_{j}")
                    def fm(en, j=j, wap=wap, q=q):
                        last = None
                        for tt in range(CH // 128):
                            for hf in range(2):
                                last = en.matmul(
                                    pb[2 * tt + hf][:, 0:512],
                                    zq[:, j * CH + tt * 128: j * CH + tt * 128 + 128],
                                    wap[:, hf * 512: hf * 512 + 512],
                                    start=(j == 0), stop=(j == NJ - 1))
                        return last
                    kb.task("te", fm, reads=(wkey,) + tuple(f"l3zq{jj}" for jj in (j,)),
                            writes=("pb0", "pb1", "pb2", "pb3"), tag=f"l3_om{q}_{j}")
                for tt in range(CH // 128):
                    tg = q * (CH // 128) + tt
                    for hf in range(2):
                        def fe(en, tt=tt, hf=hf, tg=tg):
                            return en.tensor_copy(
                                out=br[:, tg * D + hf * 512: tg * D + hf * 512 + 512],
                                in_=pb[2 * tt + hf][:, 0:512])
                        kb.task("ve", fe, reads=(f"pb{2 * tt + hf}",), writes=(f"br{tg}",),
                                tag=f"l3_oe{q}_{tt}_{hf}")

    def emit_l3_ffn(self, seq):
        import contextlib
        nc, kb = self.nc, self.kb
        x = self.sb["x"]
        br = self.sb["br"]
        stats = self.sb["stats"]
        ident = self.sb["ident_bf"]
        pb = [self.psum[f"pb{i}"] for i in range(6)]
        I = 2816
        NIC = I // 128

        self.kb.fence()
        with contextlib.ExitStack() as es:
            xq_fm = es.enter_context(nc.sbuf_tensor(f"f3xq_{seq}", [128, 8 * 512], BF16))
            hq_fm = es.enter_context(nc.sbuf_tensor(f"f3hq_{seq}", [128, NIC * 256], BF16))
            hst = es.enter_context(nc.sbuf_tensor(f"f3h_{seq}", [128, 2 * I], FP16))
            qa = es.enter_context(nc.sbuf_tensor(f"f3a_{seq}", [128, 704], F32))
            qr = es.enter_context(nc.sbuf_tensor(f"f3r_{seq}", [128, 704], F32))
            qp = es.enter_context(nc.sbuf_tensor(f"f3p_{seq}", [128, 704], F32))
            qscl = es.enter_context(nc.sbuf_tensor(f"f3s_{seq}", [128, 132], F32))
            xf32 = es.enter_context(nc.sbuf_tensor(f"f3x_{seq}", [128, 1024], F32))
            xqt = es.enter_context(nc.sbuf_tensor(f"f3xt_{seq}", [128, 1024], BF16))
            hqt = es.enter_context(nc.sbuf_tensor(f"f3ht_{seq}", [128, 1408], BF16))
            wg = es.enter_context(nc.sbuf_tensor(f"f3wg_{seq}", [128, 4096], BF16))
            wu = es.enter_context(nc.sbuf_tensor(f"f3wu_{seq}", [128, 4096], BF16))
            wd = es.enter_context(nc.sbuf_tensor(f"f3wd_{seq}", [128, 1024], BF16))
            sil = self.sb["lnscr"]
            qs = {"a": qa, "r": qr, "scl": qscl, "p": qp}
            qkeys = ("f3qa", "f3qr", "f3qs")

            gq = self.wgq_d.rearrange("(ci p) f -> p ci f", p=128)
            uq = self.wuq_d.rearrange("(ci p) f -> p ci f", p=128)
            dq = self.wdq_d.rearrange("(ci p) f -> p ci f", p=128)
            for tc in range(4):
                for i in range(4):
                    t = tc * 4 + i
                    def fx(en, t=t):
                        return en.activation(out=xf32[:], in_=x[:, t * D:(t + 1) * D],
                                             func=AF.Identity,
                                             bias=stats[:, 48 + t:49 + t],
                                             scale=stats[:, 32 + t:33 + t])
                    kb.task("se", fx, reads=(self.xkey(t), "stB2", "stB"),
                            writes=("f3xf",), tag=f"f3_x{t}")
                    for hq2 in range(2):
                        self.emit_quant(f"f3x{t}_{hq2}", xf32[:, hq2 * 512: hq2 * 512 + 512],
                                        ("f3xf",), xqt[:, hq2 * 512: hq2 * 512 + 512],
                                        ("f3xqt",), 512, qs, qkeys)
                    pt = self.psum[f"pt{i % 2}"]
                    ptk = f"pt{i % 2}"
                    def ftr(en, pt=pt):
                        last = None
                        for c in range(8):
                            last = en.transpose(pt[:, c * 128:(c + 1) * 128],
                                                xqt[:, c * 128:(c + 1) * 128], ident[:])
                        return last
                    kb.task("te", ftr, reads=("f3xqt",), writes=(ptk,), tag=f"f3_tr{t}")
                    def fev(en, i=i, pt=pt):
                        full = xq_fm[:]
                        out = bass.AP(full.tensor, i * 128, [full.ap[0], [512, 8], [1, 128]])
                        return en.tensor_copy(out=out, in_=pt[:, 0:1024])
                    kb.task("ve", fev, reads=(ptk,), writes=(f"f3xq{i}",), tag=f"f3_te{t}")
                for ttp in range(2):
                    for ic in range(6):
                        w_ic = 512 if ic < 5 else 256
                        def fwg(en, ic=ic, w_ic=w_ic):
                            return [en.dma_start(out=wg[:, 0:8 * w_ic].rearrange("p (ci o) -> p ci o", ci=8),
                                                 in_=gq[:, :, ic * 512: ic * 512 + w_ic]),
                                    en.dma_start(out=wu[:, 0:8 * w_ic].rearrange("p (ci o) -> p ci o", ci=8),
                                                 in_=uq[:, :, ic * 512: ic * 512 + w_ic])]
                        fwg.n_dma = 2
                        kb.dma_task("ge", fwg, writes=("f3wgu",), tag=f"f3_wg{tc}_{ttp}_{ic}")
                        for ti in range(2):
                            tt = ttp * 2 + ti
                            psg, psgk = pb[0 + ti], f"pb{0 + ti}"
                            psu, psuk = pb[2 + ti], f"pb{2 + ti}"
                            def fmg(en, tt=tt, w_ic=w_ic, psg=psg, psu=psu):
                                for ci in range(8):
                                    en.matmul(psg[:, 0:w_ic],
                                              xq_fm[:, ci * 512 + tt * 128: ci * 512 + tt * 128 + 128],
                                              wg[:, ci * w_ic:(ci + 1) * w_ic],
                                              start=(ci == 0), stop=(ci == 7))
                                last = None
                                for ci in range(8):
                                    last = en.matmul(psu[:, 0:w_ic],
                                                     xq_fm[:, ci * 512 + tt * 128: ci * 512 + tt * 128 + 128],
                                                     wu[:, ci * w_ic:(ci + 1) * w_ic],
                                                     start=(ci == 0), stop=(ci == 7))
                                return last
                            kb.task("te", fmg,
                                    reads=("f3wgu",) + tuple(f"f3xq{i}" for i in range(4)),
                                    writes=(psgk, psuk), tag=f"f3_gm{tc}_{ttp}_{ic}_{ti}")
                            def fsl(en, psg=psg, w_ic=w_ic):
                                return en.activation(out=sil[:, 0:w_ic], in_=psg[:, 0:w_ic],
                                                     func=AF.Silu)
                            kb.task("se", fsl, reads=(psgk,), writes=("lnscr",),
                                    tag=f"f3_sl{tc}_{ttp}_{ic}_{ti}")
                            def fh(en, ti=ti, ic=ic, w_ic=w_ic, psu=psu):
                                return en.tensor_tensor(out=hst[:, ti * I + ic * 512: ti * I + ic * 512 + w_ic],
                                                        in0=sil[:, 0:w_ic], in1=psu[:, 0:w_ic],
                                                        op=ALU.mult)
                            kb.task("ve", fh, reads=("lnscr", psuk), writes=(f"f3h{ti}",),
                                    tag=f"f3_h{tc}_{ttp}_{ic}_{ti}")
                    for ti in range(2):
                        for hq4 in range(4):
                            self.emit_quant(f"f3hq{tc}_{ttp}_{ti}_{hq4}",
                                            hst[:, ti * I + hq4 * 704: ti * I + hq4 * 704 + 704],
                                            (f"f3h{ti}",),
                                            hqt[:, hq4 * 704 if hq4 < 2 else (hq4 - 2) * 704:
                                                (hq4 * 704 if hq4 < 2 else (hq4 - 2) * 704) + 704],
                                            ("f3hqt",), 704, qs, qkeys)
                            if hq4 in (1, 3):
                                lo = 0 if hq4 == 1 else 8
                                nch = 11
                                for half in range(2):
                                    clo = half * 6
                                    chi = min(11, clo + 6)
                                    pt = self.psum[f"pt{half % 2}"]
                                    ptk = f"pt{half % 2}"
                                    def ftr(en, clo=clo, chi=chi, pt=pt):
                                        last = None
                                        for c in range(clo, chi):
                                            last = en.transpose(
                                                pt[:, (c - clo) * 128:(c - clo) * 128 + 128],
                                                hqt[:, c * 128:(c + 1) * 128], ident[:])
                                        return last
                                    kb.task("te", ftr, reads=("f3hqt",), writes=(ptk,),
                                            tag=f"f3_htr{tc}_{ttp}_{ti}_{hq4}_{half}")
                                    def fev(en, ti=ti, hq4=hq4, clo=clo, chi=chi, pt=pt):
                                        base_ic = 0 if hq4 == 1 else 11
                                        full = hq_fm[:]
                                        out = bass.AP(full.tensor,
                                                      (base_ic + clo) * 256 + ti * 128,
                                                      [full.ap[0], [256, chi - clo], [1, 128]])
                                        return en.tensor_copy(out=out,
                                                              in_=pt[:, 0:(chi - clo) * 128])
                                    kb.task("ve", fev, reads=(ptk,), writes=(f"f3hf{ti}",),
                                            tag=f"f3_hte{tc}_{ttp}_{ti}_{hq4}_{half}")
                    for icd in range(NIC):
                        sl = icd % 2
                        def fw(en, icd=icd):
                            return [en.dma_start(out=wd[:, 0:1024], in_=dq[:, icd, :])]
                        kb.dma_task("ge", fw, writes=("f3wd0",), tag=f"f3_dw{tc}_{ttp}_{icd}")
                        def fm(en, icd=icd):
                            last = None
                            for ti in range(2):
                                for hf in range(2):
                                    last = en.matmul(
                                        pb[2 * ti + hf][:, 0:512],
                                        hq_fm[:, icd * 256 + ti * 128: icd * 256 + ti * 128 + 128],
                                        wd[:, hf * 512: hf * 512 + 512],
                                        start=(icd == 0), stop=(icd == NIC - 1))
                            return last
                        kb.task("te", fm, reads=("f3wd0", "f3hf0", "f3hf1"),
                                writes=("pb0", "pb1", "pb2", "pb3"),
                                tag=f"f3_dm{tc}_{ttp}_{icd}")
                    for ti in range(2):
                        tg = tc * 4 + ttp * 2 + ti
                        for hf in range(2):
                            psk = f"pb{2 * ti + hf}"
                            def fsg(en, ti=ti, hf=hf):
                                return en.activation(out=sil[:, 0:512],
                                                     in_=pb[2 * ti + hf][:, 0:512],
                                                     func=AF.Silu)
                            kb.task("se", fsg, reads=(psk,), writes=("lnscr",),
                                    tag=f"f3_xg{tc}_{ttp}_{ti}_{hf}")
                            def fmul(en, tg=tg, hf=hf):
                                return en.tensor_tensor(
                                    out=sil[:, 0:512], in0=sil[:, 0:512],
                                    in1=br[:, tg * D + hf * 512: tg * D + hf * 512 + 512],
                                    op=ALU.mult)
                            kb.task("ve", fmul, reads=("lnscr", f"br{tg}"), writes=("lnscr",),
                                    tag=f"f3_xm{tc}_{ttp}_{ti}_{hf}")
                            def fadd(en, tg=tg, hf=hf):
                                return en.tensor_tensor(
                                    out=x[:, tg * D + hf * 512: tg * D + hf * 512 + 512],
                                    in0=x[:, tg * D + hf * 512: tg * D + hf * 512 + 512],
                                    in1=sil[:, 0:512], op=ALU.add)
                            kb.task("ve", fadd, reads=("lnscr", self.xkey(tg)),
                                    writes=(self.xkey(tg),), tag=f"f3_xa{tc}_{ttp}_{ti}_{hf}")


def build_full(host_map, nseq=SEQ_PER_CORE, stop_after=None, dump_br=False):
    nc = bass.Bass("TRN2", target_bir_lowering=False, debug=False)
    import contextlib
    dbg = (("dbg_br", [128, NT * D], BF16),) if dump_br else ()
    bd = Build(nc, host_map, debug_tensors=dbg)
    with contextlib.ExitStack() as es:
        bd.alloc_persist(es)
        bd.load_consts()
        bd.emit_l3_park()
        steps = ["l0a", "l0f", "l1a", "l1f", "l2a", "l2f", "l3a", "l3f"]
        emitters = {"l0a": bd.emit_l0_attn, "l0f": bd.emit_l0_ffn,
                    "l1a": bd.emit_l1_attn, "l1f": bd.emit_l1_ffn,
                    "l2a": bd.emit_l2_attn, "l2f": bd.emit_l2_ffn,
                    "l3a": bd.emit_l3_attn, "l3f": bd.emit_l3_ffn}
        for seq in range(nseq):
            bd.load_x(seq)
            for st in steps:
                emitters[st](seq)
                if stop_after == st:
                    break
            if dump_br:
                def fd(en):
                    return [en.dma_start(out=bd.debug["dbg_br"], in_=bd.sb["br"][:])]
                bd.kb.dma_task("sq", fd, reads=tuple(f"br{t}" for t in range(NT)))
            bd.store_out(seq)
        bd.kb.emit()
    return nc





_CACHED = {}


def kernel(**inputs):
    maps = host_prep(inputs)
    nc = build_full(maps[0], nseq=SEQ_PER_CORE)
    res = run_bass_kernel_spmd(nc, maps, core_ids=list(range(NCORES)))
    out = np.concatenate([np.asarray(r["out"]).reshape(SEQ_PER_CORE, T, D)
                          for r in res.results], axis=0)
    return np.ascontiguousarray(out).astype(np.float32)
